# version 02 (final)
# baseline (speedup 1.0000x reference)
import contextlib

import numpy as np
import ml_dtypes

import concourse.bass as bass
import concourse.mybir as mybir
from concourse.bass_utils import run_bass_kernel_spmd

F32 = mybir.dt.float32
BF16 = mybir.dt.bfloat16
U8 = mybir.dt.uint8
I32 = mybir.dt.int32
AF = mybir.ActivationFunctionType
ALU = mybir.AluOpType
AX = mybir.AxisListType

NCORES = 8
B, S, D = 2, 8192, 1024
TPC = 2048
NT = TPC // 128
NG = D // 128
NTB = TPC // 512
LN_EPS = 1e-5


def bcast_col(col_ap, n):
    return bass.AP(col_ap.tensor, col_ap.offset, [col_ap.ap[0], [0, n]])


def cm(v):
    v = np.asarray(v).reshape(-1)
    n = v.shape[0] // 128
    return np.ascontiguousarray(v.reshape(n, 128).T)


class Sem:
    def __init__(self, h):
        self.h = h
        self.n = 0


class Prog:

    ENGINES = ("sync", "tensor", "vector", "scalar", "gpsimd")

    def __init__(self, name="k"):
        self.nc = bass.Bass("TRN2", target_bir_lowering=False)
        self.stack = contextlib.ExitStack()
        self.steps = []
        self._semi = 0

    def sem(self, name=None):
        h = self.stack.enter_context(
            self.nc.semaphore(name or f"s{self._semi}"))
        self._semi += 1
        return Sem(h)

    def sbuf(self, name, shape, dt):
        return self.stack.enter_context(self.nc.sbuf_tensor(name, shape, dt))

    def psum(self, name, shape, dt=F32):
        return self.stack.enter_context(self.nc.psum_tensor(name, shape, dt))

    def dram(self, name, shape, dt, kind="Internal", **kw):
        return self.nc.dram_tensor(name, shape, dt, kind=kind, **kw)

    def on(self, eng, body, wait=(), inc=None):
        wvals = []
        for w in wait:
            if isinstance(w, tuple):
                wvals.append((w[0].h, w[1]))
            else:
                wvals.append((w.h, w.n))
        if inc is not None:
            if not isinstance(inc, tuple):
                inc = (inc, 1)
            inc[0].n += inc[1]
            inc_h, inc_k = inc[0].h, inc[1]
        else:
            inc_h, inc_k = None, 0

        def fn(e):
            for h, v in wvals:
                if v > 0:
                    e.wait_ge(h, v)
            ins = body(e)
            if inc_h is not None:
                assert ins is not None, "inc on a body returning None"
                ins.then_inc(inc_h, inc_k)
        self.steps.append((eng, fn))

    def barrier_wait(self, eng, sem, val=None):
        v = sem.n if val is None else val
        if v > 0:
            self.steps.append((eng, lambda e, h=sem.h, v=v: e.wait_ge(h, v)))

    def finalize(self):
        with self.nc.Block() as block:
            for eng in self.ENGINES:
                def run(e, eng=eng):
                    for en, fn in self.steps:
                        if en == eng:
                            fn(e)
                getattr(block, eng)(run)
        self.stack.close()
        return self.nc



class Common:

    def __init__(self, P):
        self.P = P
        self.identb = P.sbuf("identb", [128, 128], BF16)
        self.identf = P.sbuf("identf", [128, 128], F32)
        self.ones_col_b = P.sbuf("ones_col_b", [128, 1], BF16)
        self.ones_row_b = P.sbuf("ones_row_b", [1, 128], BF16)
        self.sW = P.sem("sW")
        self.sX = P.sem("sX")
        self.sPE = P.sem("sPE")
        self.sVE = P.sem("sVE")
        self.sACT = P.sem("sACT")
        self.sPL = P.sem("sPL")
        self.sOUT = P.sem("sOUT")
        self.sCC = P.sem("sCC")

        def setup(g):
            g.memset(self.identb[:], 0.0)
            g.affine_select(out=self.identb[:], in_=self.identb[:],
                            compare_op=ALU.not_equal, fill=1.0, base=0,
                            pattern=[[-1, 128]], channel_multiplier=1)
            g.memset(self.identf[:], 0.0)
            g.affine_select(out=self.identf[:], in_=self.identf[:],
                            compare_op=ALU.not_equal, fill=1.0, base=0,
                            pattern=[[-1, 128]], channel_multiplier=1)
            g.memset(self.ones_col_b[:], 1.0)
            return g.memset(self.ones_row_b[:], 1.0)
        P.on("gpsimd", setup, inc=self.sPL)


    def load_cast(self, hbm_ap, sbuf_ap, eng="gpsimd", wait=()):
        self.P.on(eng, lambda e: e.dma_start(out=sbuf_ap, in_=hbm_ap),
                  wait=wait, inc=(self.sW, 16))

    def load(self, hbm_ap, sbuf_ap, eng="sync", sem=None, wait=()):
        self.P.on(eng, lambda e: e.dma_start(out=sbuf_ap, in_=hbm_ap),
                  wait=wait, inc=(sem or self.sX, 16))

    def store(self, hbm_ap, sbuf_ap, eng="sync", wait=()):
        self.P.on(eng, lambda e: e.dma_start(out=hbm_ap, in_=sbuf_ap),
                  wait=wait, inc=(self.sOUT, 16))


    def load_w_direct(self, hbm, sbuf_t, din, dout):
        nic = din // 128
        self.load_cast(
            hbm.rearrange("(ic p) o -> p ic o", p=128),
            sbuf_t[:].rearrange("p (ic o) -> p ic o", ic=nic))

    def load_w_transposed(self, hbm, sbuf_t, dout, din, stage, pst):
        P = self.P
        for orc in range(dout // 128):
            self.load_cast(hbm[orc * 128:(orc + 1) * 128, :], stage[:, 0:din],
                           wait=[self.sPE, self.sACT, self.sVE])
            for ic in range(din // 128):
                P.on("tensor",
                     lambda e, ic=ic: e.transpose(
                         pst[:, :], stage[:, ic * 128:(ic + 1) * 128],
                         self.identb[:]),
                     wait=[self.sW, self.sVE, self.sPL], inc=self.sPE)
                P.on("vector",
                     lambda e, orc=orc, ic=ic: e.tensor_copy(
                         sbuf_t[:, ic * dout + orc * 128:
                                ic * dout + (orc + 1) * 128], pst[:, :]),
                     wait=[self.sPE], inc=self.sVE)


    def ln_tile(self, x_tile, y_tile, sq_scr, cols, wait_extra=()):
        P = self.P
        s_ = cols[:, 0:1]
        ss = cols[:, 1:2]
        m_ = cols[:, 2:3]
        v_ = cols[:, 3:4]
        r_ = cols[:, 4:5]
        w = list(wait_extra)
        P.on("scalar", lambda e: e.activation(sq_scr, x_tile, AF.Square,
                                              accum_out=ss),
             wait=[self.sVE, self.sX] + w, inc=self.sACT)
        P.on("vector", lambda e: e.tensor_reduce(s_, x_tile, AX.X, ALU.add),
             wait=[self.sX] + w, inc=self.sVE)
        P.on("vector", lambda e: e.tensor_scalar_mul(m_, s_, 1.0 / D),
             wait=[self.sVE], inc=self.sVE)
        P.on("vector", lambda e: e.tensor_scalar_mul(v_, ss, 1.0 / D),
             wait=[self.sACT, self.sVE], inc=self.sVE)
        P.on("vector", lambda e: e.tensor_tensor(r_, m_, m_, ALU.mult),
             wait=[self.sVE], inc=self.sVE)
        P.on("vector", lambda e: e.tensor_tensor(v_, v_, r_, ALU.subtract),
             wait=[self.sVE], inc=self.sVE)
        P.on("vector", lambda e: e.tensor_scalar_add(v_, v_, LN_EPS),
             wait=[self.sVE], inc=self.sVE)
        P.on("scalar", lambda e: e.activation(r_, v_, AF.Sqrt),
             wait=[self.sVE], inc=self.sACT)
        P.on("vector", lambda e: e.reciprocal(r_, r_),
             wait=[self.sACT], inc=self.sVE)
        P.on("vector", lambda e: e.scalar_tensor_tensor(
            y_tile, x_tile, m_, bcast_col(r_, D), ALU.subtract, ALU.mult),
             wait=[self.sVE], inc=self.sVE)

    def add_back(self, fT_f32, xres, scale, pstf, tstride=TPC):
        P = self.P
        for i in range(NT):
            for g in range(NG):
                P.on("tensor", lambda e, i=i, g=g: e.transpose(
                    pstf[:, :],
                    fT_f32[:, g * tstride + i * 128:
                           g * tstride + (i + 1) * 128],
                    self.identf[:]), wait=[self.sVE], inc=self.sPE)
                P.on("vector", lambda e, i=i, g=g: e.scalar_tensor_tensor(
                    xres[:, i * D + g * 128: i * D + (g + 1) * 128],
                    pstf[:, :], scale,
                    xres[:, i * D + g * 128: i * D + (g + 1) * 128],
                    ALU.mult, ALU.add), wait=[self.sPE], inc=self.sVE)



def build_ffn0_pre():
    P = Prog()
    x = P.dram("x", [TPC, D], F32, kind="ExternalInput").ap()
    gpart = P.dram("gpart", [1, D], F32, kind="ExternalOutput").ap()
    C = Common(P)
    xres = P.sbuf("xres", [128, NT * D], F32)
    yf = P.sbuf("yf", [128, D], F32)
    sq = P.sbuf("sq", [128, D], F32)
    cols = P.sbuf("cols", [128, 8], F32)
    acc = P.sbuf("acc", [128, D], F32)
    accb = P.sbuf("accb", [128, D], BF16)
    accp = P.psum("accp", [1, 512], F32)
    outr = P.sbuf("outr", [1, D], F32)

    C.load(x.rearrange("(i p) d -> p i d", p=128),
           xres[:].rearrange("p (i d) -> p i d", i=NT))
    P.on("gpsimd", lambda e: e.memset(acc[:], 0.0), inc=C.sPL)
    for i in range(NT):
        C.ln_tile(xres[:, i * D:(i + 1) * D], yf[:], sq[:], cols)
        P.on("vector", lambda e: e.tensor_tensor(acc[:], acc[:], yf[:],
                                                 ALU.add),
             wait=[C.sPL, C.sVE], inc=C.sVE)
    P.on("vector", lambda e: e.tensor_copy(accb[:], acc[:]),
         wait=[C.sVE], inc=C.sVE)
    for h in range(2):
        P.on("tensor", lambda e, h=h: e.matmul(
            accp[:, :], C.ones_col_b[:, 0:1],
            accb[:, h * 512:(h + 1) * 512], start=True, stop=True),
             wait=[C.sVE, C.sPL], inc=C.sPE)
        P.on("vector", lambda e, h=h: e.tensor_copy(
            outr[0:1, h * 512:(h + 1) * 512], accp[0:1, :]),
             wait=[C.sPE], inc=C.sVE)
    P.on("sync", lambda e: e.dma_start(out=gpart, in_=outr[0:1, :]),
         wait=[C.sVE], inc=(C.sOUT, 16))
    P.barrier_wait("sync", C.sOUT)
    return P


def build_ffn0_main(ncycles=3):
    P = Prog()
    x = P.dram("x", [TPC, D], F32, kind="ExternalInput").ap()
    cw1 = P.dram("cn_w1", [D, D], F32, kind="ExternalInput").ap()
    cw2 = P.dram("cn_w2", [D, 512], F32, kind="ExternalInput").ap()
    cw3 = P.dram("cn_w3", [512, 256], F32, kind="ExternalInput").ap()
    cw4 = P.dram("cn_w4", [128, 2], F32, kind="ExternalInput").ap()
    gw1 = P.dram("gc_w1", [D, 512], F32, kind="ExternalInput").ap()
    gw2 = P.dram("gc_w2", [512, 256], F32, kind="ExternalInput").ap()
    gw3 = P.dram("gc_w3", [128, 2], F32, kind="ExternalInput").ap()
    cb1 = P.dram("cn_b1", [128, 8], F32, kind="ExternalInput").ap()
    cb2 = P.dram("cn_b2", [128, 4], F32, kind="ExternalInput").ap()
    cb3 = P.dram("cn_b3", [128, 2], F32, kind="ExternalInput").ap()
    gb1 = P.dram("gc_b1", [128, 4], F32, kind="ExternalInput").ap()
    gb2 = P.dram("gc_b2", [128, 2], F32, kind="ExternalInput").ap()
    smal_in = P.dram("smal", [1, 4], F32, kind="ExternalInput").ap()
    b2r_in = P.dram("bsel2r", [1, 2], F32, kind="ExternalInput").ap()
    b2c_in = P.dram("bsel2c", [2, 1], F32, kind="ExternalInput").ap()
    xout = P.dram("xout", [TPC, D], F32, kind="ExternalOutput").ap()

    C = Common(P)
    cw1t = P.sbuf("cw1t", [128, 8 * D], BF16)
    cw2t = P.sbuf("cw2t", [128, 8 * 512], BF16)
    cw3t = P.sbuf("cw3t", [128, 4 * 256], BF16)
    cw4t = P.sbuf("cw4t", [128, 2 * 1], BF16)
    gw1t = P.sbuf("gw1t", [128, 8 * 512], BF16)
    gw2t = P.sbuf("gw2t", [128, 4 * 256], BF16)
    gw3t = P.sbuf("gw3t", [128, 2 * 1], BF16)
    C.load_w_direct(cw1, cw1t, D, D)
    C.load_w_direct(cw2, cw2t, D, 512)
    C.load_w_direct(cw3, cw3t, 512, 256)
    C.load_cast(cw4, cw4t[:, :])
    C.load_w_direct(gw1, gw1t, D, 512)
    C.load_w_direct(gw2, gw2t, 512, 256)
    C.load_cast(gw3, gw3t[:, :])
    bias1 = P.sbuf("bias1", [128, 8], F32)
    bias2 = P.sbuf("bias2", [128, 4], F32)
    bias3 = P.sbuf("bias3", [128, 2], F32)
    gbias1 = P.sbuf("gbias1", [128, 4], F32)
    gbias2 = P.sbuf("gbias2", [128, 2], F32)
    C.load(cb1, bias1[:, :])
    C.load(cb2, bias2[:, :])
    C.load(cb3, bias3[:, :])
    C.load(gb1, gbias1[:, :])
    C.load(gb2, gbias2[:, :])
    smal = P.sbuf("smal_t", [1, 4], F32)
    C.load(smal_in, smal[0:1, :])
    b2r = P.sbuf("b2r", [1, 2], BF16)
    C.load_cast(b2r_in, b2r[0:1, :])
    b2c = P.sbuf("b2c", [2, 1], F32)
    C.load(b2c_in, b2c[0:2, :])
    cin_h = P.dram("gsum_in", [2, D], F32)
    car_h = P.dram("gsum_red", [2, D], F32)
    acc = P.sbuf("acc", [128, D], F32)
    accb = P.sbuf("accb", [128, D], BF16)
    srow = P.sbuf("srow", [1, D], BF16)
    partsb = P.sbuf("partsb", [2, D], F32)
    carsb = P.sbuf("carsb", [2, D], F32)

    xt = P.sbuf("xt", [128, D], F32)
    outT = P.sbuf("outT", [128, NG * TPC], F32)
    outbf = P.sbuf("outbf", [128, NG * 512], BF16)
    h1T = P.sbuf("h1T", [128, 8 * 512], BF16)
    h2T = P.sbuf("h2T", [128, 4 * 512], BF16)
    h3T = P.sbuf("h3T", [128, 2 * 512], BF16)
    tw = P.sbuf("tw", [128, 2 * 512], F32)
    cplrow = P.sbuf("cplrow", [1, 512], F32)
    cplb = P.sbuf("cplb", [1, 512], BF16)
    ab_sc = P.sbuf("ab_sc", [1, 8], F32)
    gcol = P.sbuf("gcol", [128, NG], F32)
    gcolb = P.sbuf("gcolb", [128, NG], BF16)
    gc_h1 = P.sbuf("gc_h1", [128, 4], F32)
    gc_h1b = P.sbuf("gc_h1b", [128, 4], BF16)
    gc_h2 = P.sbuf("gc_h2", [128, 2], F32)
    gc_h2b = P.sbuf("gc_h2b", [128, 2], BF16)
    sq = P.sbuf("sq", [128, D], F32)
    cols = P.sbuf("cols", [128, 8], F32)
    yf = P.sbuf("yf", [128, D], F32)

    ps_m = [P.psum(f"ps_m{j}", [128, 512], F32) for j in range(4)]
    ps_r = P.psum("ps_r", [1, 512], F32)
    ps_b = P.psum("ps_b", [128, 512], F32)
    ps_g = P.psum("ps_g", [128, 8], F32)
    ps_tf = P.psum("ps_tf", [128, 128], F32)


    P.on("gpsimd", lambda e: e.memset(acc[:], 0.0), inc=C.sPL)
    for i in range(NT):
        C.load(x[i * 128:(i + 1) * 128, :], xt[:],
               wait=[C.sVE, C.sACT])
        C.ln_tile(xt[:], yf[:], sq[:], cols)
        for g in range(NG):
            P.on("tensor", lambda e, g=g: e.transpose(
                ps_tf[:, :], yf[:, g * 128:(g + 1) * 128], C.identf[:]),
                 wait=[C.sVE], inc=C.sPE)
            P.on("vector", lambda e, i=i, g=g: e.tensor_copy(
                outT[:, g * TPC + i * 128: g * TPC + (i + 1) * 128],
                ps_tf[:, :]), wait=[C.sPE], inc=C.sVE)

    P.on("vector", lambda e: e.tensor_copy(accb[:], acc[:]),
         wait=[C.sVE], inc=C.sVE)
    for h in range(2):
        P.on("tensor", lambda e, h=h: e.matmul(
            ps_r[:, :], C.ones_col_b[:, 0:1],
            accb[:, h * 512:(h + 1) * 512], start=True, stop=True),
             wait=[C.sVE, C.sPL], inc=C.sPE)
        P.on("vector", lambda e, h=h: e.tensor_copy(
            srow[0:1, h * 512:(h + 1) * 512], ps_r[0:1, :]),
             wait=[C.sPE, C.sVE], inc=C.sVE)
    for h in range(2):
        P.on("tensor", lambda e, h=h: e.matmul(
            ps_b[0:2, :], b2r[0:1, 0:2],
            srow[0:1, h * 512:(h + 1) * 512], start=True, stop=True),
             wait=[C.sVE, C.sW], inc=C.sPE)
        P.on("vector", lambda e, h=h: e.tensor_copy(
            partsb[0:2, h * 512:(h + 1) * 512], ps_b[0:2, :]),
             wait=[C.sPE, C.sVE], inc=C.sVE)
    P.on("gpsimd", lambda e: e.dma_start(out=cin_h.ap(), in_=partsb[0:2, :]),
         wait=[C.sVE], inc=(C.sCC, 16))
    P.on("gpsimd", lambda e: e.collective_compute(
        "AllReduce", ALU.add, replica_groups=[list(range(NCORES))],
        ins=[cin_h.ap().opt()], outs=[car_h.ap().opt()]),
         wait=[C.sCC], inc=C.sCC)
    P.on("gpsimd", lambda e: e.dma_start(out=carsb[0:2, :], in_=car_h.ap()),
         wait=[C.sCC], inc=(C.sX, 16))
    for g in range(NG):
        P.on("tensor", lambda e, g=g: e.matmul(
            ps_g[:, 6:7], carsb[0:2, g * 128:(g + 1) * 128],
            b2c[0:2, 0:1], start=True, stop=True),
             wait=[C.sVE, C.sW, C.sX], inc=C.sPE)
        P.on("vector", lambda e, g=g: e.tensor_scalar_mul(
            gcol[:, g:g + 1], ps_g[:, 6:7], 1.0 / S),
             wait=[C.sPE], inc=C.sVE)
    P.on("vector", lambda e: e.tensor_copy(gcolb[:, :], gcol[:, :]),
         wait=[C.sVE], inc=C.sVE)
    for ot in range(4):
        for icc in range(8):
            P.on("tensor", lambda e, ot=ot, icc=icc: e.matmul(
                ps_g[:, ot:ot + 1],
                gw1t[:, icc * 512 + ot * 128: icc * 512 + (ot + 1) * 128],
                gcolb[:, icc:icc + 1], start=(icc == 0), stop=(icc == 7)),
                 wait=[C.sW, C.sVE], inc=C.sPE)
        P.on("scalar", lambda e, ot=ot: e.activation(
            gc_h1[:, ot:ot + 1], ps_g[:, ot:ot + 1], AF.Gelu,
            bias=gbias1[:, ot:ot + 1]), wait=[C.sPE, C.sX], inc=C.sACT)
    P.on("vector", lambda e: e.tensor_copy(gc_h1b[:, :], gc_h1[:, :]),
         wait=[C.sACT], inc=C.sVE)
    for ot in range(2):
        for icc in range(4):
            P.on("tensor", lambda e, ot=ot, icc=icc: e.matmul(
                ps_g[:, 4 + ot:5 + ot],
                gw2t[:, icc * 256 + ot * 128: icc * 256 + (ot + 1) * 128],
                gc_h1b[:, icc:icc + 1], start=(icc == 0), stop=(icc == 3)),
                 wait=[C.sVE], inc=C.sPE)
        P.on("scalar", lambda e, ot=ot: e.activation(
            gc_h2[:, ot:ot + 1], ps_g[:, 4 + ot:5 + ot], AF.Gelu,
            bias=gbias2[:, ot:ot + 1]), wait=[C.sPE], inc=C.sACT)
    P.on("vector", lambda e: e.tensor_copy(gc_h2b[:, :], gc_h2[:, :]),
         wait=[C.sACT], inc=C.sVE)
    for icc in range(2):
        P.on("tensor", lambda e, icc=icc: e.matmul(
            ps_g[0:1, 6:7], gw3t[:, icc:icc + 1], gc_h2b[:, icc:icc + 1],
            start=(icc == 0), stop=(icc == 1)),
             wait=[C.sVE], inc=C.sPE)
    P.on("scalar", lambda e: e.activation(ab_sc[0:1, 0:1], ps_g[0:1, 6:7],
                                          AF.Sigmoid, bias=smal[0:1, 1:2]),
         wait=[C.sPE, C.sX], inc=C.sACT)
    P.on("vector", lambda e: e.tensor_scalar_mul(ab_sc[0:1, 1:2],
                                                 smal[0:1, 2:3], 0.6),
         wait=[C.sX], inc=C.sVE)
    P.on("vector", lambda e: e.tensor_tensor(ab_sc[0:1, 2:3],
                                             ab_sc[0:1, 0:1],
                                             smal[0:1, 2:3], ALU.mult),
         wait=[C.sACT], inc=C.sVE)
    P.on("vector", lambda e: e.tensor_scalar(ab_sc[0:1, 3:4],
                                             ab_sc[0:1, 2:3],
                                             1.4, 0.1, ALU.mult, ALU.add),
         wait=[C.sVE], inc=C.sVE)
    P.on("vector", lambda e: e.tensor_tensor(ab_sc[0:1, 3:4],
                                             ab_sc[0:1, 3:4],
                                             smal[0:1, 2:3], ALU.subtract),
         wait=[C.sVE], inc=C.sVE)
    twist_src = [4, 5, 6, 7, 0, 1, 2, 3]
    twist_sgn = [1.0, 1.0, -1.0, -1.0, -1.0, -1.0, 1.0, 1.0]

    for cyc in range(ncycles):
        for tb in range(NTB):
            t0 = tb * 512
            for g in range(NG):
                P.on("vector", lambda e, g=g, t0=t0: e.tensor_copy(
                    outbf[:, g * 512:(g + 1) * 512],
                    outT[:, g * TPC + t0: g * TPC + t0 + 512]),
                     wait=[C.sVE, C.sPE], inc=C.sVE)
            for og in range(8):
                pm = ps_m[og % 4]
                for ic in range(8):
                    P.on("tensor", lambda e, og=og, ic=ic, pm=pm: e.matmul(
                        pm[:, :],
                        cw1t[:, ic * D + og * 128: ic * D + (og + 1) * 128],
                        outbf[:, ic * 512:(ic + 1) * 512],
                        start=(ic == 0), stop=(ic == 7)),
                         wait=[C.sW, C.sVE, C.sACT], inc=C.sPE)
                P.on("scalar", lambda e, og=og, pm=pm: e.activation(
                    h1T[:, og * 512:(og + 1) * 512], pm[:, :], AF.Gelu,
                    bias=bias1[:, og:og + 1]), wait=[C.sPE], inc=C.sACT)
            for og in range(4):
                pm = ps_m[og % 4]
                for ic in range(8):
                    P.on("tensor", lambda e, og=og, ic=ic, pm=pm: e.matmul(
                        pm[:, :],
                        cw2t[:, ic * 512 + og * 128:
                             ic * 512 + (og + 1) * 128],
                        h1T[:, ic * 512:(ic + 1) * 512],
                        start=(ic == 0), stop=(ic == 7)),
                         wait=[C.sACT], inc=C.sPE)
                P.on("scalar", lambda e, og=og, pm=pm: e.activation(
                    h2T[:, og * 512:(og + 1) * 512], pm[:, :], AF.Gelu,
                    bias=bias2[:, og:og + 1]), wait=[C.sPE], inc=C.sACT)
            for og in range(2):
                pm = ps_m[og % 4]
                for ic in range(4):
                    P.on("tensor", lambda e, og=og, ic=ic, pm=pm: e.matmul(
                        pm[:, :],
                        cw3t[:, ic * 256 + og * 128:
                             ic * 256 + (og + 1) * 128],
                        h2T[:, ic * 512:(ic + 1) * 512],
                        start=(ic == 0), stop=(ic == 3)),
                         wait=[C.sACT], inc=C.sPE)
                P.on("scalar", lambda e, og=og, pm=pm: e.activation(
                    h3T[:, og * 512:(og + 1) * 512], pm[:, :], AF.Gelu,
                    bias=bias3[:, og:og + 1]), wait=[C.sPE], inc=C.sACT)
            for ic in range(2):
                P.on("tensor", lambda e, ic=ic: e.matmul(
                    ps_r[:, :], cw4t[:, ic:ic + 1],
                    h3T[:, ic * 512:(ic + 1) * 512],
                    start=(ic == 0), stop=(ic == 1)),
                     wait=[C.sACT, C.sVE], inc=C.sPE)
            P.on("scalar", lambda e: e.activation(
                cplrow[0:1, :], ps_r[0:1, :], AF.Sigmoid,
                bias=smal[0:1, 0:1]), wait=[C.sPE], inc=C.sACT)
            P.on("vector", lambda e: e.tensor_scalar(
                cplrow[0:1, :], cplrow[0:1, :], ab_sc[0:1, 1:2],
                ab_sc[0:1, 3:4], ALU.mult, ALU.add),
                 wait=[C.sACT], inc=C.sVE)
            P.on("vector", lambda e: e.tensor_copy(cplb[0:1, :],
                                                   cplrow[0:1, :]),
                 wait=[C.sVE], inc=C.sVE)
            P.on("tensor", lambda e: e.matmul(
                ps_b[:, :], C.ones_row_b[0:1, :], cplb[0:1, :],
                start=True, stop=True), wait=[C.sVE], inc=C.sPE)
            for gp_ in range(4):
                ga, gb_ = gp_, gp_ + 4
                P.on("vector", lambda e, ga=ga, gb_=gb_, t0=t0:
                     e.scalar_tensor_tensor(
                         tw[:, 0:512],
                         outT[:, gb_ * TPC + t0: gb_ * TPC + t0 + 512],
                         twist_sgn[ga], ps_b[:, :], ALU.mult, ALU.mult),
                     wait=[C.sPE, C.sVE], inc=C.sVE)
                P.on("vector", lambda e, ga=ga, gb_=gb_, t0=t0:
                     e.scalar_tensor_tensor(
                         tw[:, 512:1024],
                         outT[:, ga * TPC + t0: ga * TPC + t0 + 512],
                         twist_sgn[gb_], ps_b[:, :], ALU.mult, ALU.mult),
                     wait=[C.sVE], inc=C.sVE)
                P.on("vector", lambda e, ga=ga, t0=t0: e.tensor_tensor(
                    outT[:, ga * TPC + t0: ga * TPC + t0 + 512],
                    outT[:, ga * TPC + t0: ga * TPC + t0 + 512],
                    tw[:, 0:512], ALU.add), wait=[C.sVE], inc=C.sVE)
                P.on("vector", lambda e, gb_=gb_, t0=t0: e.tensor_tensor(
                    outT[:, gb_ * TPC + t0: gb_ * TPC + t0 + 512],
                    outT[:, gb_ * TPC + t0: gb_ * TPC + t0 + 512],
                    tw[:, 512:1024], ALU.add), wait=[C.sVE], inc=C.sVE)

    for i in range(NT):
        C.load(x[i * 128:(i + 1) * 128, :], xt[:],
               wait=[C.sOUT, C.sVE])
        for g in range(NG):
            P.on("tensor", lambda e, i=i, g=g: e.transpose(
                ps_tf[:, :],
                outT[:, g * TPC + i * 128: g * TPC + (i + 1) * 128],
                C.identf[:]), wait=[C.sVE], inc=C.sPE)
            P.on("vector", lambda e, g=g: e.scalar_tensor_tensor(
                xt[:, g * 128:(g + 1) * 128], ps_tf[:, :], 0.5,
                xt[:, g * 128:(g + 1) * 128], ALU.mult, ALU.add),
                 wait=[C.sPE, C.sX], inc=C.sVE)
        C.store(xout[i * 128:(i + 1) * 128, :], xt[:], wait=[C.sVE])
    P.barrier_wait("gpsimd", C.sOUT)
    return P



_CACHE = {}


def _run(builder, in_maps, name):
    if name not in _CACHE:
        _CACHE[name] = builder().finalize()
    return run_bass_kernel_spmd(_CACHE[name], in_maps,
                                core_ids=list(range(NCORES)))


def _shard_x(x):
    return [np.ascontiguousarray(
        x[c // 4, (c % 4) * TPC:(c % 4 + 1) * TPC, :].astype(np.float32))
        for c in range(NCORES)]


def _gather_x(shards):
    out = np.empty((B, S, D), np.float32)
    for c in range(NCORES):
        out[c // 4, (c % 4) * TPC:(c % 4 + 1) * TPC, :] = shards[c]
    return out




def build_ssm_pre():
    P = Prog()
    x = P.dram("x", [TPC, D], F32, kind="ExternalInput").ap()
    bw = P.dram("B_w", [D, D], F32, kind="ExternalInput").ap()
    bb = P.dram("B_b", [128, 8], F32, kind="ExternalInput").ap()
    la = P.dram("log_A", [128, 8], F32, kind="ExternalInput").ap()
    hT_out = P.dram("hT", [D, TPC], BF16, kind="ExternalOutput").ap()
    end_out = P.dram("endst", [128, 8], F32, kind="ExternalOutput").ap()
    C = Common(P)
    bwT = P.sbuf("bwT", [128, 8 * D], BF16)
    stage = P.sbuf("stage", [128, D], BF16)
    pst = P.psum("pst", [128, 128], BF16)
    C.load_w_transposed(bw, bwT, D, D, stage, pst)
    bbcol = P.sbuf("bbcol", [128, 8], F32)
    C.load(bb, bbcol[:, :])
    lacol = P.sbuf("lacol", [128, 8], F32)
    C.load(la, lacol[:, :])
    acol = P.sbuf("acol", [128, 8], F32)
    P.on("scalar", lambda e: e.activation(acol[:, :], lacol[:, :],
                                          AF.Sigmoid),
         wait=[C.sX], inc=C.sACT)

    xt = P.sbuf("xt", [128, D], F32)
    ybf = P.sbuf("ybf", [128, D], BF16)
    yf32 = P.sbuf("yf32", [128, D], F32)
    yT32 = P.sbuf("yT32", [128, D], F32)
    sq = P.sbuf("sq", [128, D], F32)
    cols = P.sbuf("cols", [128, 8], F32)
    yT = P.sbuf("yT", [128, NG * TPC], BF16)
    bxT = P.sbuf("bxT", [128, NG * TPC], F32)
    hT = P.sbuf("hT_t", [128, NG * TPC], BF16)
    endc = P.sbuf("endc", [128, 8], F32)
    ps_b = [P.psum(f"ps_b{j}", [128, 512], F32) for j in range(2)]
    ps_tb = P.psum("ps_tb", [128, 128], BF16)

    for i in range(NT):
        C.load(x[i * 128:(i + 1) * 128, :], xt[:], wait=[C.sVE, C.sACT])
        C.ln_tile(xt[:], ybf[:], sq[:], cols)
        for g in range(NG):
            P.on("tensor", lambda e, g=g: e.transpose(
                ps_tb[:, :], ybf[:, g * 128:(g + 1) * 128], C.identb[:]),
                 wait=[C.sVE], inc=C.sPE)
            P.on("vector", lambda e, i=i, g=g: e.tensor_copy(
                yT[:, g * TPC + i * 128: g * TPC + (i + 1) * 128],
                ps_tb[:, :]), wait=[C.sPE], inc=C.sVE)
    for ng in range(NG):
        for tb in range(NTB):
            pm = ps_b[tb % 2]
            for ic in range(NG):
                P.on("tensor", lambda e, ng=ng, tb=tb, ic=ic, pm=pm: e.matmul(
                    pm[:, :],
                    bwT[:, ic * D + ng * 128: ic * D + (ng + 1) * 128],
                    yT[:, ic * TPC + tb * 512: ic * TPC + tb * 512 + 512],
                    start=(ic == 0), stop=(ic == NG - 1)),
                     wait=[C.sW, C.sVE], inc=C.sPE)
            P.on("vector", lambda e, ng=ng, tb=tb, pm=pm: e.tensor_scalar(
                bxT[:, ng * TPC + tb * 512: ng * TPC + tb * 512 + 512],
                pm[:, :], bbcol[:, ng:ng + 1], 0.0, ALU.add, ALU.add),
                 wait=[C.sPE, C.sX], inc=C.sVE)
    for ng in range(NG):
        P.on("vector", lambda e, ng=ng: e.tensor_tensor_scan(
            hT[:, ng * TPC:(ng + 1) * TPC],
            bcast_col(acol[:, ng:ng + 1], TPC),
            bxT[:, ng * TPC:(ng + 1) * TPC], 0.0, ALU.mult, ALU.add),
             wait=[C.sVE, C.sACT], inc=C.sVE)
        P.on("vector", lambda e, ng=ng: e.tensor_copy(
            endc[:, ng:ng + 1], hT[:, (ng + 1) * TPC - 1:(ng + 1) * TPC]),
             wait=[C.sVE], inc=C.sVE)
    C.store(hT_out.rearrange("(g p) t -> p g t", p=128),
            hT[:].rearrange("p (g t) -> p g t", g=NG), wait=[C.sVE])
    C.store(end_out, endc[:, :], wait=[C.sVE])
    P.barrier_wait("gpsimd", C.sOUT)
    return P


def build_ssm_main():
    P = Prog()
    x = P.dram("x", [TPC, D], F32, kind="ExternalInput").ap()
    hT_in = P.dram("hT", [D, TPC], BF16, kind="ExternalInput").ap()
    carry = P.dram("carry", [128, 8], F32, kind="ExternalInput").ap()
    la = P.dram("log_A", [128, 8], F32, kind="ExternalInput").ap()
    cw = P.dram("C_w", [D, D], F32, kind="ExternalInput").ap()
    cb = P.dram("C_b", [128, 8], F32, kind="ExternalInput").ap()
    xout = P.dram("xout", [TPC, D], F32, kind="ExternalOutput").ap()
    C = Common(P)
    cwT = P.sbuf("cwT", [128, 8 * D], BF16)
    stage = P.sbuf("stage", [128, D], BF16)
    pst = P.psum("pst", [128, 128], BF16)
    C.load_w_transposed(cw, cwT, D, D, stage, pst)
    cbcol = P.sbuf("cbcol", [128, 8], F32)
    C.load(cb, cbcol[:, :])
    lacol = P.sbuf("lacol", [128, 8], F32)
    C.load(la, lacol[:, :])
    ccol = P.sbuf("ccol", [128, 8], F32)
    C.load(carry, ccol[:, :])
    lnacol = P.sbuf("lnacol", [128, 8], F32)
    P.on("scalar", lambda e: e.activation(lnacol[:, :], lacol[:, :],
                                          AF.Sigmoid), wait=[C.sX],
         inc=C.sACT)
    P.on("scalar", lambda e: e.activation(lnacol[:, :], lnacol[:, :],
                                          AF.Ln), wait=[C.sACT],
         inc=C.sACT)
    hT = P.sbuf("hT_t", [128, NG * TPC], BF16)
    C.load(hT_in.rearrange("(g p) t -> p g t", p=128),
           hT[:].rearrange("p (g t) -> p g t", g=NG))
    iot = P.sbuf("iot", [128, TPC], F32)
    ang = P.sbuf("ang", [128, TPC], F32)
    pw = P.sbuf("pw", [128, TPC], F32)
    P.on("gpsimd", lambda e: e.iota(iot[:, :], pattern=[[1, TPC]], base=1,
                                    channel_multiplier=0,
                                    allow_small_or_imprecise_dtypes=True),
         inc=C.sPL)
    hc = P.sbuf("hc", [128, NG * TPC], BF16)
    for ng in range(NG):
        P.on("vector", lambda e, ng=ng: e.tensor_scalar(
            ang[:, :], iot[:, :], lnacol[:, ng:ng + 1], 0.0, ALU.mult,
            ALU.add), wait=[C.sPL, C.sACT, C.sVE], inc=C.sVE)
        P.on("scalar", lambda e: e.activation(pw[:, :], ang[:, :], AF.Exp),
             wait=[C.sVE, C.sACT], inc=C.sACT)
        P.on("vector", lambda e, ng=ng: e.scalar_tensor_tensor(
            hc[:, ng * TPC:(ng + 1) * TPC], pw[:, :], ccol[:, ng:ng + 1],
            hT[:, ng * TPC:(ng + 1) * TPC], ALU.mult, ALU.add),
             wait=[C.sACT, C.sX, C.sW], inc=C.sVE)
    fT = P.sbuf("fT", [128, NG * 512], F32)
    xt = P.sbuf("xt", [128, D], F32)
    ps_m = [P.psum(f"ps_m{j}", [128, 512], F32) for j in range(2)]
    ps_tf = P.psum("ps_tf", [128, 128], F32)
    for tb in range(NTB):
        for dg in range(NG):
            pm = ps_m[dg % 2]
            for nc_ in range(NG):
                P.on("tensor", lambda e, tb=tb, dg=dg, nc_=nc_, pm=pm:
                     e.matmul(
                         pm[:, :],
                         cwT[:, nc_ * D + dg * 128: nc_ * D + (dg + 1) * 128],
                         hc[:, nc_ * TPC + tb * 512:
                            nc_ * TPC + tb * 512 + 512],
                         start=(nc_ == 0), stop=(nc_ == NG - 1)),
                     wait=[C.sW, C.sVE], inc=C.sPE)
            P.on("vector", lambda e, dg=dg, pm=pm: e.tensor_scalar(
                fT[:, dg * 512:(dg + 1) * 512], pm[:, :],
                cbcol[:, dg:dg + 1], 0.0, ALU.add, ALU.add),
                 wait=[C.sPE, C.sX], inc=C.sVE)
        for ii in range(4):
            i = tb * 4 + ii
            C.load(x[i * 128:(i + 1) * 128, :], xt[:],
                   wait=[C.sVE, C.sOUT])
            for g in range(NG):
                P.on("tensor", lambda e, ii=ii, g=g: e.transpose(
                    ps_tf[:, :],
                    fT[:, g * 512 + ii * 128: g * 512 + (ii + 1) * 128],
                    C.identf[:]), wait=[C.sVE], inc=C.sPE)
                P.on("vector", lambda e, g=g: e.scalar_tensor_tensor(
                    xt[:, g * 128:(g + 1) * 128], ps_tf[:, :], 1.0,
                    xt[:, g * 128:(g + 1) * 128], ALU.mult, ALU.add),
                     wait=[C.sPE, C.sX], inc=C.sVE)
            C.store(xout[i * 128:(i + 1) * 128, :], xt[:], wait=[C.sVE])
    P.barrier_wait("gpsimd", C.sOUT)
    return P


def ssm_layer(xs, w, pfx):
    in1 = []
    for c in range(NCORES):
        in1.append({"x": xs[c], "B_w": w[f"{pfx}_B_w"],
                    "B_b": cm(w[f"{pfx}_B_b"]),
                    "log_A": cm(w[f"{pfx}_log_A"])})
    r = _run(build_ssm_pre, in1, "ssm_pre")
    hTs = [r.results[c]["hT"] for c in range(NCORES)]
    ends = [r.results[c]["endst"] for c in range(NCORES)]
    in2 = []
    for c in range(NCORES):
        zc = np.zeros((128, 8), np.float32)
        carry = zc if c % 4 == 0 else ends[c - 1]
        in2.append({"x": xs[c], "hT": hTs[c], "carry": carry,
                    "log_A": cm(w[f"{pfx}_log_A"]),
                    "C_w": w[f"{pfx}_C_w"], "C_b": cm(w[f"{pfx}_C_b"])})
    r = _run(build_ssm_main, in2, "ssm_main")
    return [r.results[c]["xout"] for c in range(NCORES)]




def build_moe1():
    P = Prog()
    x = P.dram("x", [TPC, D], F32, kind="ExternalInput").ap()
    rw = P.dram("router_w", [D, 8], F32, kind="ExternalInput").ap()
    upw = P.dram("up_w", [7 * D, 4096], F32, kind="ExternalInput").ap()
    gw = P.dram("gate_w", [7 * D, 4096], F32, kind="ExternalInput").ap()
    dw = P.dram("down_w", [7 * 4096, D], F32, kind="ExternalInput").ap()
    sup = P.dram("shared_up_w", [D, 4096], F32, kind="ExternalInput").ap()
    sgw = P.dram("shared_gate_w", [D, 4096], F32, kind="ExternalInput").ap()
    sdw = P.dram("shared_down_w", [4096, D], F32, kind="ExternalInput").ap()
    xout = P.dram("xout", [TPC, D], F32, kind="ExternalOutput").ap()
    C = Common(P)
    H = 4096
    NH = H // 128

    rwt = P.sbuf("rwt", [128, 8 * 8], F32)
    C.load(rw.rearrange("(ic p) o -> p ic o", p=128),
           rwt[:].rearrange("p (ic o) -> p ic o", ic=NG))

    xt = P.sbuf("xt", [128, D], F32)
    ybf = P.sbuf("ybf", [128, D], BF16)
    yf32 = P.sbuf("yf32", [128, D], F32)
    yT32 = P.sbuf("yT32", [128, D], F32)
    sq = P.sbuf("sq", [128, D], F32)
    cols = P.sbuf("cols", [128, 8], F32)
    yT = P.sbuf("yT", [128, NG * TPC], BF16)
    accT = P.sbuf("accT", [128, NG * TPC], F32)
    hidT = P.sbuf("hidT", [128, NH * 512], BF16)
    ustrip = [P.sbuf(f"ustrip{j}", [128, NG * 128], BF16) for j in range(2)]
    gstrip = [P.sbuf(f"gstrip{j}", [128, NG * 128], BF16) for j in range(2)]
    dstrip = [P.sbuf(f"dstrip{j}", [128, D], BF16) for j in range(2)]
    gbc = P.sbuf("gbc", [128, 512], F32)
    gtmp = P.sbuf("gtmp", [128, 512], F32)
    grows = P.sbuf("grows", [8, TPC], F32)
    gbcrow = P.sbuf("gbcrow", [1, 512], F32)
    ones_row_f = P.sbuf("ones_row_f", [1, 128], F32)
    silu_s = P.sbuf("silu_s", [128, 512], F32)
    prob = P.sbuf("prob", [128, 8], F32)
    pcols = P.sbuf("pcols", [128, 8], F32)
    g8 = P.sbuf("g8", [128, 8], F32)

    ps8 = [P.psum(f"ps8_{j}", [128, 512], F32) for j in range(8)]
    pstb = ps8[7][:, 0:64].bitcast(BF16)
    pstf = ps8[7][:, 128:256]
    P.on("gpsimd", lambda e: e.memset(ones_row_f[:], 1.0), inc=C.sPL)

    for i in range(NT):
        C.load(x[i * 128:(i + 1) * 128, :], xt[:], wait=[C.sVE, C.sACT])
        C.ln_tile(xt[:], yf32[:], sq[:], cols)
        P.on("vector", lambda e: e.tensor_copy(ybf[:, :], yf32[:, :]),
             wait=[C.sVE], inc=C.sVE)
        for g in range(NG):
            P.on("tensor", lambda e, i=i, g=g: e.transpose(
                pstb[:, :], ybf[:, g * 128:(g + 1) * 128], C.identb[:]),
                 wait=[C.sVE], inc=C.sPE)
            P.on("vector", lambda e, i=i, g=g: e.tensor_copy(
                yT[:, g * TPC + i * 128: g * TPC + (i + 1) * 128],
                pstb[:, :]), wait=[C.sPE], inc=C.sVE)
            P.on("tensor", lambda e, g=g: e.transpose(
                pstf[:, :], yf32[:, g * 128:(g + 1) * 128], C.identf[:]),
                 wait=[C.sVE], inc=C.sPE)
            P.on("vector", lambda e, g=g: e.tensor_copy(
                yT32[:, g * 128:(g + 1) * 128], pstf[:, :]),
                 wait=[C.sPE], inc=C.sVE)
        for ic in range(NG):
            P.on("tensor", lambda e, i=i, ic=ic: e.matmul(
                ps8[0][:, 0:8],
                yT32[:, ic * 128:(ic + 1) * 128],
                rwt[:, ic * 8:(ic + 1) * 8],
                start=(ic == 0), stop=(ic == NG - 1)),
                 wait=[C.sW, C.sVE], inc=C.sPE)
        P.on("vector", lambda e: e.tensor_reduce(
            pcols[:, 0:1], ps8[0][:, 0:8], AX.X, ALU.max, negate=True),
             wait=[C.sPE], inc=C.sVE)
        P.on("scalar", lambda e: e.activation(
            prob[:, :], ps8[0][:, 0:8], AF.Exp, bias=pcols[:, 0:1]),
             wait=[C.sVE], inc=C.sACT)
        P.on("vector", lambda e: e.tensor_reduce(
            pcols[:, 1:2], prob[:, :], AX.X, ALU.add),
             wait=[C.sACT], inc=C.sVE)
        P.on("vector", lambda e: e.reciprocal(pcols[:, 2:3], pcols[:, 1:2]),
             wait=[C.sVE], inc=C.sVE)
        P.on("vector", lambda e: e.tensor_scalar(
            prob[:, :], prob[:, :], pcols[:, 2:3], 0.0, ALU.mult, ALU.add),
             wait=[C.sVE], inc=C.sVE)
        P.on("vector", lambda e: e.tensor_reduce(
            pcols[:, 4:5], prob[:, :], AX.X, ALU.max),
             wait=[C.sVE], inc=C.sVE)
        P.on("vector", lambda e: e.tensor_tensor(
            g8[:, :], prob[:, :], bcast_col(pcols[:, 4:5], 8), ALU.is_ge),
             wait=[C.sVE], inc=C.sVE)
        P.on("vector", lambda e: e.tensor_scalar(
            g8[:, :], g8[:, :], pcols[:, 4:5], 2.0, ALU.mult, ALU.mult),
             wait=[C.sVE], inc=C.sVE)
        P.on("tensor", lambda e, i=i: e.transpose(
            pstf[0:8, 0:128], g8[:, :], C.identf[:]), wait=[C.sVE],
             inc=C.sPE)
        P.on("vector", lambda e, i=i: e.tensor_copy(
            grows[0:8, i * 128:(i + 1) * 128], pstf[0:8, 0:128]),
             wait=[C.sPE], inc=C.sVE)
    P.on("gpsimd", lambda e: e.memset(accT[:, :], 0.0),
         wait=[C.sVE], inc=C.sPL)

    for ee in range(8):
        routed = ee < 7
        for tb in range(NTB):
            t0 = tb * 512
            if routed:
                P.on("gpsimd", lambda e, ee=ee, t0=t0: e.dma_start(
                    out=gbcrow[0:1, :],
                    in_=grows[ee:ee + 1, t0:t0 + 512]),
                     wait=[C.sVE, C.sPE], inc=(C.sX, 16))
                P.on("tensor", lambda e: e.matmul(
                    ps8[6][:, :], ones_row_f[0:1, :], gbcrow[0:1, :],
                    start=True, stop=True),
                     wait=[C.sX, C.sVE, C.sPL], inc=C.sPE)
                P.on("vector", lambda e: e.tensor_copy(gbc[:, :],
                                                       ps8[6][:, :]),
                     wait=[C.sPE, C.sVE], inc=C.sVE)
            src = yT
            sstride = TPC
            soff = t0
            for og in range(NH):
                sb = og % 2
                if routed:
                    C.load_cast(
                        upw[ee * D:(ee + 1) * D,
                            og * 128:(og + 1) * 128].rearrange(
                            "(ic p) o -> p ic o", p=128),
                        ustrip[sb][:].rearrange("p (ic o) -> p ic o", ic=NG),
                        wait=[C.sPE])
                    C.load_cast(
                        gw[ee * D:(ee + 1) * D,
                           og * 128:(og + 1) * 128].rearrange(
                            "(ic p) o -> p ic o", p=128),
                        gstrip[sb][:].rearrange("p (ic o) -> p ic o", ic=NG),
                        wait=[C.sPE])
                else:
                    C.load_cast(
                        sup[:, og * 128:(og + 1) * 128].rearrange(
                            "(ic p) o -> p ic o", p=128),
                        ustrip[sb][:].rearrange("p (ic o) -> p ic o", ic=NG),
                        wait=[C.sPE])
                    C.load_cast(
                        sgw[:, og * 128:(og + 1) * 128].rearrange(
                            "(ic p) o -> p ic o", p=128),
                        gstrip[sb][:].rearrange("p (ic o) -> p ic o", ic=NG),
                        wait=[C.sPE])
                pu, pg = ps8[og % 2], ps8[2 + og % 2]
                for ic in range(NG):
                    P.on("tensor", lambda e, og=og, ic=ic, sb=sb, pu=pu,
                         src=src, soff=soff, sstride=sstride: e.matmul(
                         pu[:, :], ustrip[sb][:, ic * 128:(ic + 1) * 128],
                         src[:, ic * sstride + soff: ic * sstride + soff + 512],
                         start=(ic == 0), stop=(ic == NG - 1)),
                         wait=[C.sW, C.sVE, C.sACT], inc=C.sPE)
                for ic in range(NG):
                    P.on("tensor", lambda e, og=og, ic=ic, sb=sb, pg=pg,
                         src=src, soff=soff, sstride=sstride: e.matmul(
                         pg[:, :], gstrip[sb][:, ic * 128:(ic + 1) * 128],
                         src[:, ic * sstride + soff: ic * sstride + soff + 512],
                         start=(ic == 0), stop=(ic == NG - 1)),
                         wait=[C.sW, C.sVE, C.sACT], inc=C.sPE)
                P.on("scalar", lambda e, pg=pg: e.activation(
                    silu_s[:, :], pg[:, :], AF.Silu),
                     wait=[C.sPE], inc=C.sACT)
                P.on("vector", lambda e, og=og, pu=pu: e.tensor_tensor(
                    hidT[:, og * 512:(og + 1) * 512], pu[:, :], silu_s[:, :],
                    ALU.mult), wait=[C.sACT, C.sPE], inc=C.sVE)
            for hc in range(NH):
                sb = hc % 2
                if routed:
                    C.load_cast(dw[ee * H + hc * 128: ee * H + (hc + 1) * 128,
                                   :], dstrip[sb][:, :], wait=[C.sPE])
                else:
                    C.load_cast(sdw[hc * 128:(hc + 1) * 128, :],
                                dstrip[sb][:, :], wait=[C.sPE])
                for dg in range(NG):
                    P.on("tensor", lambda e, hc=hc, dg=dg, sb=sb: e.matmul(
                        ps8[dg][:, :],
                        dstrip[sb][:, dg * 128:(dg + 1) * 128],
                        hidT[:, hc * 512:(hc + 1) * 512],
                        start=(hc == 0), stop=(hc == NH - 1)),
                         wait=[C.sW, C.sVE], inc=C.sPE)
            for dg in range(NG):
                if routed:
                    P.on("vector", lambda e, dg=dg: e.tensor_tensor(
                        gtmp[:, :], ps8[dg][:, :], gbc[:, :], ALU.mult),
                         wait=[C.sPE, C.sVE, C.sPL], inc=C.sVE)
                    P.on("vector", lambda e, dg=dg, t0=t0: e.tensor_tensor(
                        accT[:, dg * TPC + t0: dg * TPC + t0 + 512],
                        accT[:, dg * TPC + t0: dg * TPC + t0 + 512],
                        gtmp[:, :], ALU.add),
                         wait=[C.sVE], inc=C.sVE)
                else:
                    P.on("vector", lambda e, dg=dg, t0=t0: e.tensor_tensor(
                        accT[:, dg * TPC + t0: dg * TPC + t0 + 512],
                        accT[:, dg * TPC + t0: dg * TPC + t0 + 512],
                        ps8[dg][:, :], ALU.add),
                         wait=[C.sPE, C.sVE, C.sPL], inc=C.sVE)

    for i in range(NT):
        C.load(x[i * 128:(i + 1) * 128, :], xt[:], wait=[C.sVE, C.sOUT])
        for g in range(NG):
            P.on("tensor", lambda e, i=i, g=g: e.transpose(
                pstf[:, :],
                accT[:, g * TPC + i * 128: g * TPC + (i + 1) * 128],
                C.identf[:]), wait=[C.sVE], inc=C.sPE)
            P.on("vector", lambda e, g=g: e.scalar_tensor_tensor(
                xt[:, g * 128:(g + 1) * 128], pstf[:, :], 0.5,
                xt[:, g * 128:(g + 1) * 128], ALU.mult, ALU.add),
                 wait=[C.sPE, C.sX], inc=C.sVE)
        C.store(xout[i * 128:(i + 1) * 128, :], xt[:], wait=[C.sVE])
    P.barrier_wait("gpsimd", C.sOUT)
    return P


def moe1_layer(xs, w, pfx):
    in_maps = []
    for c in range(NCORES):
        in_maps.append({
            "x": xs[c],
            "router_w": w[f"{pfx}_router_w"],
            "up_w": w[f"{pfx}_up_w"].reshape(-1, 4096),
            "gate_w": w[f"{pfx}_gate_w"].reshape(-1, 4096),
            "down_w": w[f"{pfx}_down_w"].reshape(-1, D),
            "shared_up_w": w[f"{pfx}_shared_up_w"],
            "shared_gate_w": w[f"{pfx}_shared_gate_w"],
            "shared_down_w": w[f"{pfx}_shared_down_w"],
        })
    r = _run(build_moe1, in_maps, "moe1")
    return [r.results[c]["xout"] for c in range(NCORES)]




def build_tffn2():
    P = Prog()
    H = 4096
    NH = H // 128
    x = P.dram("x", [TPC, D], F32, kind="ExternalInput").ap()
    w1 = P.dram("w1_w", [H, D], F32, kind="ExternalInput").ap()
    w2 = P.dram("w2_w", [H, D], F32, kind="ExternalInput").ap()
    h1 = P.dram("h1_w", [H, H], F32, kind="ExternalInput").ap()
    h2 = P.dram("h2_w", [H, H], F32, kind="ExternalInput").ap()
    w3 = P.dram("w3_w", [D, H], F32, kind="ExternalInput").ap()
    s1 = P.dram("w1_s", [128, NH], F32, kind="ExternalInput").ap()
    s2 = P.dram("w2_s", [128, NH], F32, kind="ExternalInput").ap()
    sh1 = P.dram("h1_s", [128, NH], F32, kind="ExternalInput").ap()
    sh2 = P.dram("h2_s", [128, NH], F32, kind="ExternalInput").ap()
    s3 = P.dram("w3_s", [128, NG], F32, kind="ExternalInput").ap()
    xout = P.dram("xout", [TPC, D], F32, kind="ExternalOutput").ap()
    C = Common(P)

    s1c = P.sbuf("s1c", [128, NH], F32)
    s2c = P.sbuf("s2c", [128, NH], F32)
    sh1c = P.sbuf("sh1c", [128, NH], F32)
    sh2c = P.sbuf("sh2c", [128, NH], F32)
    s3c = P.sbuf("s3c", [128, NG], F32)
    C.load(s1, s1c[:, :])
    C.load(s2, s2c[:, :])
    C.load(sh1, sh1c[:, :])
    C.load(sh2, sh2c[:, :])
    C.load(s3, s3c[:, :])

    xt = P.sbuf("xt", [128, D], F32)
    ybf = P.sbuf("ybf", [128, D], BF16)
    cols = P.sbuf("cols", [128, 8], F32)
    yT = P.sbuf("yT", [128, NG * TPC], BF16)
    bufA = P.sbuf("bufA", [128, NH * 1024], BF16)
    bufB = P.sbuf("bufB", [128, NH * 1024], BF16)
    stage = P.sbuf("stage", [128, H], BF16)
    strip1 = [P.sbuf("strip1_0", [128, NG * 128], BF16)] * 2
    strip2 = [P.sbuf("strip2_0", [128, NG * 128], BF16)] * 2
    striph = [P.sbuf("striph_0", [128, NH * 128], BF16)] * 2
    silu_s = P.sbuf("silu_s", [128, 512], F32)
    t2s = P.sbuf("t2s", [128, 512], F32)
    fT = P.sbuf("fT", [128, NG * 512], F32)
    sq = stage[:].bitcast(F32)[:, 0:D]

    ps = [P.psum(f"ps{j}", [128, 512], F32) for j in range(6)]
    pst = P.psum("pst", [128, 128], BF16)
    pstf = P.psum("pstf", [128, 128], F32)

    for i in range(NT):
        C.load(x[i * 128:(i + 1) * 128, :], xt[:], wait=[C.sVE, C.sACT])
        C.ln_tile(xt[:], ybf[:], sq, cols)
        for g in range(NG):
            P.on("tensor", lambda e, g=g: e.transpose(
                pst[:, :], ybf[:, g * 128:(g + 1) * 128], C.identb[:]),
                 wait=[C.sVE], inc=C.sPE)
            P.on("vector", lambda e, i=i, g=g: e.tensor_copy(
                yT[:, g * TPC + i * 128: g * TPC + (i + 1) * 128],
                pst[:, :]), wait=[C.sPE], inc=C.sVE)

    for half in range(2):
        toff = half * 1024
        for og in range(NH):
            sb = og % 2
            C.load_w_transposed(w1[og * 128:(og + 1) * 128, :], strip1[sb],
                                128, D, stage, pst)
            C.load_w_transposed(w2[og * 128:(og + 1) * 128, :], strip2[sb],
                                128, D, stage, pst)
            for tb in range(2):
                t0 = toff + tb * 512
                p1, p2 = ps[tb], ps[2 + tb]
                for ic in range(NG):
                    P.on("tensor", lambda e, og=og, ic=ic, sb=sb, t0=t0,
                         p1=p1: e.matmul(
                         p1[:, :], strip1[sb][:, ic * 128:(ic + 1) * 128],
                         yT[:, ic * TPC + t0: ic * TPC + t0 + 512],
                         start=(ic == 0), stop=(ic == NG - 1)),
                         wait=[C.sVE, C.sACT], inc=C.sPE)
                for ic in range(NG):
                    P.on("tensor", lambda e, og=og, ic=ic, sb=sb, t0=t0,
                         p2=p2: e.matmul(
                         p2[:, :], strip2[sb][:, ic * 128:(ic + 1) * 128],
                         yT[:, ic * TPC + t0: ic * TPC + t0 + 512],
                         start=(ic == 0), stop=(ic == NG - 1)),
                         wait=[C.sVE, C.sACT], inc=C.sPE)
                P.on("scalar", lambda e, og=og, p1=p1: e.activation(
                    silu_s[:, :], p1[:, :], AF.Silu,
                    scale=s1c[:, og:og + 1]),
                     wait=[C.sPE, C.sX], inc=C.sACT)
                P.on("vector", lambda e, og=og, p2=p2: e.tensor_scalar(
                    t2s[:, :], p2[:, :], s2c[:, og:og + 1], 0.0,
                    ALU.mult, ALU.add), wait=[C.sPE, C.sX, C.sVE],
                     inc=C.sVE)
                P.on("vector", lambda e, og=og, tb=tb: e.tensor_tensor(
                    bufA[:, og * 1024 + tb * 512: og * 1024 + tb * 512 + 512],
                    silu_s[:, :], t2s[:, :], ALU.mult),
                     wait=[C.sACT, C.sVE], inc=C.sVE)
        for (hw, sc, src, dst) in [(h1, sh1c, bufA, bufB),
                                   (h2, sh2c, bufB, bufA)]:
            for og in range(NH):
                sb = og % 2
                C.load_w_transposed(hw[og * 128:(og + 1) * 128, :],
                                    striph[sb], 128, H, stage, pst)
                for tb in range(2):
                    pm = ps[4 + tb]
                    for ic in range(NH):
                        P.on("tensor", lambda e, og=og, ic=ic, sb=sb, tb=tb,
                             pm=pm, src=src: e.matmul(
                             pm[:, :],
                             striph[sb][:, ic * 128:(ic + 1) * 128],
                             src[:, ic * 1024 + tb * 512:
                                 ic * 1024 + tb * 512 + 512],
                             start=(ic == 0), stop=(ic == NH - 1)),
                             wait=[C.sVE, C.sACT], inc=C.sPE)
                    P.on("scalar", lambda e, og=og, tb=tb, pm=pm, sc=sc,
                         dst=dst: e.activation(
                         dst[:, og * 1024 + tb * 512:
                             og * 1024 + tb * 512 + 512],
                         pm[:, :], AF.Gelu, scale=sc[:, og:og + 1]),
                         wait=[C.sPE, C.sX], inc=C.sACT)
        for tb in range(2):
            t0 = toff + tb * 512
            for og in range(NG):
                sb = og % 2
                C.load_w_transposed(w3[og * 128:(og + 1) * 128, :],
                                    striph[sb], 128, H, stage, pst)
                pm = ps[og % 4]
                for ic in range(NH):
                    P.on("tensor", lambda e, og=og, ic=ic, sb=sb, tb=tb,
                         pm=pm: e.matmul(
                         pm[:, :], striph[sb][:, ic * 128:(ic + 1) * 128],
                         bufA[:, ic * 1024 + tb * 512:
                              ic * 1024 + tb * 512 + 512],
                         start=(ic == 0), stop=(ic == NH - 1)),
                         wait=[C.sVE, C.sACT], inc=C.sPE)
                P.on("vector", lambda e, og=og, pm=pm: e.tensor_scalar(
                    fT[:, og * 512:(og + 1) * 512], pm[:, :],
                    s3c[:, og:og + 1], 0.0, ALU.mult, ALU.add),
                     wait=[C.sPE, C.sX, C.sVE], inc=C.sVE)
            for ii in range(4):
                i = (t0 // 128) + ii
                C.load(x[i * 128:(i + 1) * 128, :], xt[:],
                       wait=[C.sVE, C.sOUT])
                for g in range(NG):
                    P.on("tensor", lambda e, ii=ii, g=g: e.transpose(
                        pstf[:, :],
                        fT[:, g * 512 + ii * 128: g * 512 + (ii + 1) * 128],
                        C.identf[:]), wait=[C.sVE], inc=C.sPE)
                    P.on("vector", lambda e, g=g: e.scalar_tensor_tensor(
                        xt[:, g * 128:(g + 1) * 128], pstf[:, :], 0.5,
                        xt[:, g * 128:(g + 1) * 128], ALU.mult, ALU.add),
                         wait=[C.sPE, C.sX], inc=C.sVE)
                C.store(xout[i * 128:(i + 1) * 128, :], xt[:],
                        wait=[C.sVE])
    P.barrier_wait("gpsimd", C.sOUT)
    return P


def tffn2_layer(xs, w, pfx):
    in_maps = []
    for c in range(NCORES):
        m = {"x": xs[c]}
        for k in ["w1_w", "w2_w", "h1_w", "h2_w", "w3_w"]:
            m[k] = w[f"{pfx}_{k}"]
        for k in ["w1_s", "w2_s", "h1_s", "h2_s", "w3_s"]:
            m[k] = cm(w[f"{pfx}_{k}"])
        in_maps.append(m)
    r = _run(build_tffn2, in_maps, "tffn2")
    return [r.results[c]["xout"] for c in range(NCORES)]




def build_moe3_pre():
    P = Prog()
    E = 32
    x = P.dram("x", [TPC, D], F32, kind="ExternalInput").ap()
    wg = P.dram("Wg", [D, E], F32, kind="ExternalInput").ap()
    selprob_o = P.dram("selprob", [TPC, E], F32, kind="ExternalOutput").ap()
    cumex_o = P.dram("cumex", [TPC, E], F32, kind="ExternalOutput").ap()
    totals_o = P.dram("totals", [1, E], F32, kind="ExternalOutput").ap()
    C = Common(P)
    wgt = P.sbuf("wgt", [128, NG * E], F32)
    C.load(wg.rearrange("(ic p) o -> p ic o", p=128),
           wgt[:].rearrange("p (ic o) -> p ic o", ic=NG))
    ones_row_f = P.sbuf("ones_row_f", [1, 128], F32)
    trib = P.sbuf("trib", [128, 128], BF16)

    def setup(g):
        g.memset(ones_row_f[:], 1.0)
        g.memset(trib[:], 0.0)
        return g.affine_select(out=trib[:], in_=trib[:],
                               compare_op=ALU.is_ge, fill=1.0, base=0,
                               pattern=[[-1, 128]], channel_multiplier=1)
    P.on("gpsimd", setup, inc=C.sPL)

    xt = P.sbuf("xt", [128, D], F32)
    yf32 = P.sbuf("yf32", [128, D], F32)
    yT32 = P.sbuf("yT32", [128, D], F32)
    sq = P.sbuf("sq", [128, D], F32)
    cols = P.sbuf("cols", [128, 8], F32)
    pc = P.sbuf("pc", [128, E], F32)
    prob = P.sbuf("prob", [128, E], F32)
    selp = P.sbuf("selp", [128, E], F32)
    tokc = P.sbuf("tokc", [128, E], F32)
    tokcb = P.sbuf("tokcb", [128, E], BF16)
    cume = P.sbuf("cume", [128, E], F32)
    carry = P.sbuf("carry", [1, E], F32)
    pcols = P.sbuf("pcols", [128, 8], F32)
    ohv = P.sbuf("ohv", [128, E], F32)
    selp_all = P.sbuf("selp_all", [128, NT * E], F32)
    cume_all = P.sbuf("cume_all", [128, NT * E], F32)

    ps_r = P.psum("ps_r", [128, E], F32)
    ps_c = P.psum("ps_c", [128, E], F32)
    ps_s = P.psum("ps_s", [1, E], F32)
    pstf = P.psum("pstf", [128, 128], F32)

    P.on("gpsimd", lambda e: e.memset(carry[:], 0.0), inc=C.sPL)
    for i in range(NT):
        C.load(x[i * 128:(i + 1) * 128, :], xt[:], wait=[C.sVE, C.sACT])
        C.ln_tile(xt[:], yf32[:], sq[:], cols)
        for g in range(NG):
            P.on("tensor", lambda e, g=g: e.transpose(
                pstf[:, :], yf32[:, g * 128:(g + 1) * 128], C.identf[:]),
                 wait=[C.sVE], inc=C.sPE)
            P.on("vector", lambda e, g=g: e.tensor_copy(
                yT32[:, g * 128:(g + 1) * 128], pstf[:, :]),
                 wait=[C.sPE], inc=C.sVE)
        for ic in range(NG):
            P.on("tensor", lambda e, ic=ic: e.matmul(
                ps_r[:, :], yT32[:, ic * 128:(ic + 1) * 128],
                wgt[:, ic * E:(ic + 1) * E],
                start=(ic == 0), stop=(ic == NG - 1)),
                 wait=[C.sW, C.sVE, C.sX], inc=C.sPE)
        P.on("vector", lambda e: e.tensor_reduce(
            pcols[:, 0:1], ps_r[:, :], AX.X, ALU.max, negate=True),
             wait=[C.sPE], inc=C.sVE)
        P.on("scalar", lambda e: e.activation(
            prob[:, :], ps_r[:, :], AF.Exp, bias=pcols[:, 0:1]),
             wait=[C.sVE], inc=C.sACT)
        P.on("vector", lambda e: e.tensor_reduce(
            pcols[:, 1:2], prob[:, :], AX.X, ALU.add),
             wait=[C.sACT], inc=C.sVE)
        P.on("vector", lambda e: e.reciprocal(pcols[:, 2:3], pcols[:, 1:2]),
             wait=[C.sVE], inc=C.sVE)
        P.on("vector", lambda e: e.tensor_scalar(
            prob[:, :], prob[:, :], pcols[:, 2:3], 0.0, ALU.mult, ALU.add),
             wait=[C.sVE], inc=C.sVE)
        P.on("vector", lambda e: e.tensor_copy(pc[:, :], prob[:, :]),
             wait=[C.sVE], inc=C.sVE)
        P.on("gpsimd", lambda e: e.memset(selp[:, :], 0.0),
             wait=[C.sVE], inc=C.sPL)
        P.on("gpsimd", lambda e: e.memset(tokc[:, :], 0.0),
             wait=[C.sVE], inc=C.sPL)
        for r in range(4):
            P.on("vector", lambda e: e.tensor_reduce(
                pcols[:, 3:4], pc[:, :], AX.X, ALU.max),
                 wait=[C.sVE, C.sPL], inc=C.sVE)
            P.on("vector", lambda e: e.tensor_tensor(
                ohv[:, :], pc[:, :], bcast_col(pcols[:, 3:4], E), ALU.is_ge),
                 wait=[C.sVE], inc=C.sVE)
            P.on("vector", lambda e: e.scalar_tensor_tensor(
                selp[:, :], ohv[:, :], pcols[:, 3:4], selp[:, :],
                ALU.mult, ALU.add), wait=[C.sVE, C.sPL], inc=C.sVE)
            P.on("vector", lambda e: e.tensor_tensor(
                tokc[:, :], tokc[:, :], ohv[:, :], ALU.add),
                 wait=[C.sVE, C.sPL], inc=C.sVE)
            P.on("vector", lambda e: e.scalar_tensor_tensor(
                pc[:, :], ohv[:, :], -2.0, pc[:, :], ALU.mult, ALU.add),
                 wait=[C.sVE], inc=C.sVE)
        P.on("vector", lambda e: e.tensor_copy(tokcb[:, :], tokc[:, :]),
             wait=[C.sVE], inc=C.sVE)
        P.on("tensor", lambda e: e.matmul(
            ps_c[:, :], trib[:, :], tokcb[:, :], start=True, stop=False),
             wait=[C.sVE, C.sPL], inc=C.sPE)
        P.on("tensor", lambda e: e.matmul(
            ps_c[:, :], ones_row_f[0:1, :], carry[0:1, :],
            start=False, stop=True), wait=[C.sVE, C.sPL], inc=C.sPE)
        P.on("vector", lambda e, i=i: e.tensor_copy(
            cume_all[:, i * E:(i + 1) * E], ps_c[:, :]),
             wait=[C.sPE], inc=C.sVE)
        P.on("vector", lambda e, i=i: e.tensor_copy(
            selp_all[:, i * E:(i + 1) * E], selp[:, :]),
             wait=[C.sVE], inc=C.sVE)
        P.on("tensor", lambda e: e.matmul(
            ps_s[:, :], C.ones_col_b[:, 0:1], tokcb[:, :],
            start=True, stop=True), wait=[C.sVE], inc=C.sPE)
        P.on("vector", lambda e: e.tensor_tensor(
            carry[0:1, :], carry[0:1, :], ps_s[0:1, :], ALU.add),
             wait=[C.sPE, C.sVE], inc=C.sVE)
    C.store(selprob_o.rearrange("(i p) e -> p i e", p=128),
            selp_all[:].rearrange("p (i e) -> p i e", i=NT), wait=[C.sVE])
    C.store(cumex_o.rearrange("(i p) e -> p i e", p=128),
            cume_all[:].rearrange("p (i e) -> p i e", i=NT), wait=[C.sVE])
    C.store(totals_o, carry[0:1, :], wait=[C.sVE])
    P.barrier_wait("gpsimd", C.sOUT)
    return P


def build_moe3_main():
    P = Prog()
    E = 32
    H3 = 512
    CAP = 8192.0
    x = P.dram("x", [TPC, D], F32, kind="ExternalInput").ap()
    selprob_i = P.dram("selprob", [TPC, E], F32, kind="ExternalInput").ap()
    cumex_i = P.dram("cumex", [TPC, E], F32, kind="ExternalInput").ap()
    totals_i = P.dram("totals", [NCORES, E], F32, kind="ExternalInput").ap()
    prevsel_i = P.dram("prevsel", [NCORES, 1], F32,
                       kind="ExternalInput").ap()
    w1 = P.dram("W1", [E * D, H3], F32, kind="ExternalInput").ap()
    w2 = P.dram("W2", [E * H3, D], F32, kind="ExternalInput").ap()
    w3 = P.dram("W3", [E * D, H3], F32, kind="ExternalInput").ap()
    ws1 = P.dram("Ws1", [D, D], F32, kind="ExternalInput").ap()
    ws2 = P.dram("Ws2", [D, D], F32, kind="ExternalInput").ap()
    ws3 = P.dram("Ws3", [D, D], F32, kind="ExternalInput").ap()
    xout = P.dram("xout", [TPC, D], F32, kind="ExternalOutput").ap()
    C = Common(P)
    ones_row_f = P.sbuf("ones_row_f", [1, 128], F32)
    P.on("gpsimd", lambda e: e.memset(ones_row_f[:], 1.0), inc=C.sPL)

    xt = P.sbuf("xt", [128, D], F32)
    ybf = P.sbuf("ybf", [128, D], BF16)
    sq = P.sbuf("sq", [128, D], F32)
    cols = P.sbuf("cols", [128, 8], F32)
    yT = P.sbuf("yT", [128, NG * TPC], BF16)
    accT = P.sbuf("accT", [128, NG * TPC], F32)
    tot = P.sbuf("tot", [NCORES, E], F32)
    prevs = P.sbuf("prevs", [NCORES, 1], F32)
    offs = P.sbuf("offs", [1, E], F32)
    sel_t = P.sbuf("sel_t", [128, E], F32)
    cum_t = P.sbuf("cum_t", [128, E], F32)
    msk = P.sbuf("msk", [128, E], F32)
    gts = P.sbuf("gts", [128, E], F32)
    grows = P.sbuf("grows", [E, TPC], F32)
    gbcrow = P.sbuf("gbcrow", [1, 512], F32)
    gbc = P.sbuf("gbc", [128, 512], F32)
    hidT = P.sbuf("hidT", [128, 8 * 512], BF16)
    tmpf = P.sbuf("tmpf", [128, 512], F32)
    silu_s = P.sbuf("silu_s", [128, 512], F32)
    w1t = [P.sbuf(f"w1t{j}", [128, NG * H3], BF16) for j in range(2)]
    w3t = [P.sbuf(f"w3t{j}", [128, NG * H3], BF16) for j in range(2)]
    w2t = [P.sbuf(f"w2t{j}", [128, 4 * D], BF16) for j in range(2)]
    C.load(totals_i, tot[0:NCORES, :])
    C.load(prevsel_i, prevs[0:NCORES, :])

    ps8 = [P.psum(f"ps8_{j}", [128, 512], F32) for j in range(8)]
    pstb = ps8[7][:, 0:64].bitcast(BF16)
    pstf = ps8[7][:, 128:256]

    P.on("tensor", lambda e: e.matmul(
        ps8[6][0:1, 0:E], prevs[0:NCORES, 0:1], tot[0:NCORES, :],
        start=True, stop=True), wait=[C.sX, C.sW], inc=C.sPE)
    P.on("vector", lambda e: e.tensor_copy(offs[0:1, :], ps8[6][0:1, 0:E]),
         wait=[C.sPE], inc=C.sVE)

    for i in range(NT):
        C.load(x[i * 128:(i + 1) * 128, :], xt[:], wait=[C.sVE, C.sACT])
        C.ln_tile(xt[:], ybf[:], sq[:], cols)
        for g in range(NG):
            P.on("tensor", lambda e, g=g: e.transpose(
                pstb[:, :], ybf[:, g * 128:(g + 1) * 128], C.identb[:]),
                 wait=[C.sVE], inc=C.sPE)
            P.on("vector", lambda e, i=i, g=g: e.tensor_copy(
                yT[:, g * TPC + i * 128: g * TPC + (i + 1) * 128],
                pstb[:, :]), wait=[C.sPE], inc=C.sVE)
        C.load(selprob_i[i * 128:(i + 1) * 128, :], sel_t[:, :],
               wait=[C.sVE])
        C.load(cumex_i[i * 128:(i + 1) * 128, :], cum_t[:, :],
               wait=[C.sVE])
        P.on("tensor", lambda e: e.matmul(
            ps8[6][:, 0:E], ones_row_f[0:1, :], offs[0:1, :],
            start=True, stop=True), wait=[C.sVE, C.sPL], inc=C.sPE)
        P.on("vector", lambda e: e.tensor_tensor(
            cum_t[:, :], cum_t[:, :], ps8[6][:, 0:E], ALU.add),
             wait=[C.sPE, C.sX], inc=C.sVE)
        P.on("vector", lambda e: e.tensor_scalar(
            msk[:, :], cum_t[:, :], CAP, 0.0, ALU.is_ge, ALU.add),
             wait=[C.sVE], inc=C.sVE)
        P.on("vector", lambda e: e.tensor_scalar(
            msk[:, :], msk[:, :], -1.0, 1.0, ALU.mult, ALU.add),
             wait=[C.sVE], inc=C.sVE)
        P.on("vector", lambda e: e.tensor_tensor(
            gts[:, :], sel_t[:, :], msk[:, :], ALU.mult),
             wait=[C.sVE, C.sX], inc=C.sVE)
        P.on("tensor", lambda e, i=i: e.transpose(
            pstf[0:E, 0:128], gts[:, :], C.identf[:]),
             wait=[C.sVE], inc=C.sPE)
        P.on("vector", lambda e, i=i: e.tensor_copy(
            grows[0:E, i * 128:(i + 1) * 128], pstf[0:E, 0:128]),
             wait=[C.sPE], inc=C.sVE)

    P.on("gpsimd", lambda e: e.memset(accT[:, :], 0.0),
         wait=[C.sVE], inc=C.sPL)

    for ee in range(E + 2):
        routed = ee < E
        sb = ee % 2
        if routed:
            w1src = w1[ee * D:(ee + 1) * D, :]
            w3src = w3[ee * D:(ee + 1) * D, :]
            w2src = w2[ee * H3:(ee + 1) * H3, :]
        else:
            hf = ee - E
            w1src = ws1[:, hf * H3:(hf + 1) * H3]
            w3src = ws3[:, hf * H3:(hf + 1) * H3]
            w2src = ws2[hf * H3:(hf + 1) * H3, :]
        C.load_cast(w1src.rearrange("(ic p) o -> p ic o", p=128),
                    w1t[sb][:].rearrange("p (ic o) -> p ic o", ic=NG),
                    wait=[C.sPE])
        C.load_cast(w3src.rearrange("(ic p) o -> p ic o", p=128),
                    w3t[sb][:].rearrange("p (ic o) -> p ic o", ic=NG),
                    wait=[C.sPE])
        C.load_cast(w2src.rearrange("(ic p) o -> p ic o", p=128),
                    w2t[sb][:].rearrange("p (ic o) -> p ic o", ic=4),
                    wait=[C.sPE])
        u_t, g_t, d_t = w1t[sb], w3t[sb], w2t[sb]
        hh = H3
        nog = hh // 128
        for tb in range(NTB):
            t0 = tb * 512
            if routed:
                P.on("gpsimd", lambda e, ee=ee, t0=t0: e.dma_start(
                    out=gbcrow[0:1, :], in_=grows[ee:ee + 1, t0:t0 + 512]),
                     wait=[C.sVE, C.sPE], inc=(C.sX, 16))
                P.on("tensor", lambda e: e.matmul(
                    ps8[6][:, :], ones_row_f[0:1, :], gbcrow[0:1, :],
                    start=True, stop=True),
                     wait=[C.sX, C.sVE, C.sPL], inc=C.sPE)
                P.on("vector", lambda e: e.tensor_copy(gbc[:, :],
                                                       ps8[6][:, :]),
                     wait=[C.sPE, C.sVE], inc=C.sVE)
            for og in range(nog):
                p1, p3 = ps8[og % 2], ps8[2 + og % 2]
                for ic in range(NG):
                    P.on("tensor", lambda e, og=og, ic=ic, p1=p1, u_t=u_t,
                         hh=hh, t0=t0: e.matmul(
                         p1[:, :], u_t[:, ic * hh + og * 128:
                                       ic * hh + (og + 1) * 128],
                         yT[:, ic * TPC + t0: ic * TPC + t0 + 512],
                         start=(ic == 0), stop=(ic == NG - 1)),
                         wait=[C.sW, C.sVE, C.sACT], inc=C.sPE)
                for ic in range(NG):
                    P.on("tensor", lambda e, og=og, ic=ic, p3=p3, g_t=g_t,
                         hh=hh, t0=t0: e.matmul(
                         p3[:, :], g_t[:, ic * hh + og * 128:
                                       ic * hh + (og + 1) * 128],
                         yT[:, ic * TPC + t0: ic * TPC + t0 + 512],
                         start=(ic == 0), stop=(ic == NG - 1)),
                         wait=[C.sW, C.sVE, C.sACT], inc=C.sPE)
                P.on("scalar", lambda e, p1=p1: e.activation(
                    silu_s[:, :], p1[:, :], AF.Silu),
                     wait=[C.sPE], inc=C.sACT)
                P.on("vector", lambda e, p3=p3: e.tensor_tensor(
                    tmpf[:, :], silu_s[:, :], p3[:, :], ALU.mult),
                     wait=[C.sACT, C.sPE, C.sVE], inc=C.sVE)
                if routed:
                    P.on("vector", lambda e, og=og: e.tensor_tensor(
                        hidT[:, og * 512:(og + 1) * 512], tmpf[:, :],
                        gbc[:, :], ALU.mult), wait=[C.sVE], inc=C.sVE)
                else:
                    P.on("vector", lambda e, og=og: e.tensor_copy(
                        hidT[:, og * 512:(og + 1) * 512], tmpf[:, :]),
                         wait=[C.sVE], inc=C.sVE)
            nhc = nog
            for hc in range(nhc):
                for dg in range(NG):
                    P.on("tensor", lambda e, hc=hc, dg=dg, d_t=d_t:
                         e.matmul(
                         ps8[dg][:, :], d_t[:, hc * D + dg * 128:
                                            hc * D + (dg + 1) * 128],
                         hidT[:, hc * 512:(hc + 1) * 512],
                         start=(hc == 0), stop=(hc == nhc - 1)),
                         wait=[C.sW, C.sVE], inc=C.sPE)
            for dg in range(NG):
                P.on("vector", lambda e, dg=dg, t0=t0: e.tensor_tensor(
                    accT[:, dg * TPC + t0: dg * TPC + t0 + 512],
                    accT[:, dg * TPC + t0: dg * TPC + t0 + 512],
                    ps8[dg][:, :], ALU.add),
                     wait=[C.sPE, C.sVE, C.sPL], inc=C.sVE)

    for i in range(NT):
        C.load(x[i * 128:(i + 1) * 128, :], xt[:], wait=[C.sVE, C.sOUT])
        for g in range(NG):
            P.on("tensor", lambda e, i=i, g=g: e.transpose(
                pstf[:, :],
                accT[:, g * TPC + i * 128: g * TPC + (i + 1) * 128],
                C.identf[:]), wait=[C.sVE], inc=C.sPE)
            P.on("vector", lambda e, g=g: e.scalar_tensor_tensor(
                xt[:, g * 128:(g + 1) * 128], pstf[:, :], 0.5,
                xt[:, g * 128:(g + 1) * 128], ALU.mult, ALU.add),
                 wait=[C.sPE, C.sX], inc=C.sVE)
        C.store(xout[i * 128:(i + 1) * 128, :], xt[:], wait=[C.sVE])
    P.barrier_wait("gpsimd", C.sOUT)
    return P


def moe3_layer(xs, w, pfx):
    in1 = [{"x": xs[c], "Wg": w[f"{pfx}_Wg"]} for c in range(NCORES)]
    r = _run(build_moe3_pre, in1, "moe3_pre")
    selps = [r.results[c]["selprob"] for c in range(NCORES)]
    cumes = [r.results[c]["cumex"] for c in range(NCORES)]
    totals = np.ascontiguousarray(
        np.stack([r.results[c]["totals"][0] for c in range(NCORES)], 0))
    in2 = []
    for c in range(NCORES):
        pv = np.zeros((NCORES, 1), np.float32)
        pv[:c, 0] = 1.0
        in2.append({
            "x": xs[c], "selprob": selps[c], "cumex": cumes[c],
            "totals": totals, "prevsel": pv,
            "W1": w[f"{pfx}_W1"].reshape(-1, 512),
            "W2": w[f"{pfx}_W2"].reshape(-1, D),
            "W3": w[f"{pfx}_W3"].reshape(-1, 512),
            "Ws1": w[f"{pfx}_Ws1"], "Ws2": w[f"{pfx}_Ws2"],
            "Ws3": w[f"{pfx}_Ws3"],
        })
    r = _run(build_moe3_main, in2, "moe3_main")
    return [r.results[c]["xout"] for c in range(NCORES)]




L2SCALE = float(1.0 / np.sqrt(8192.0))


def build_l2a_pre():
    P = Prog()
    x = P.dram("x", [TPC, D], F32, kind="ExternalInput").ap()
    qkvw = P.dram("qkv_w", [3 * D, D], F32, kind="ExternalInput").ap()
    qkvb = P.dram("qkv_b", [1, 3 * D], F32, kind="ExternalInput").ap()
    nw = P.dram("norm_w", [128, NG], F32, kind="ExternalInput").ap()
    nb = P.dram("norm_b", [128, NG], F32, kind="ExternalInput").ap()
    spart_o = P.dram("Spart", [D, D], F32, kind="ExternalOutput").ap()
    vt_o = P.dram("vT", [D, TPC], BF16, kind="ExternalOutput").ap()
    C = Common(P)
    qkvT = P.sbuf("qkvT", [128, NG * 3 * D], BF16)
    stage = P.sbuf("stage", [128, D], BF16)
    pst = P.psum("pst", [128, 128], BF16)
    C.load_w_transposed(qkvw, qkvT, 3 * D, D, stage, pst)
    vb = P.dram("vb", [128, NG], F32, kind="ExternalInput").ap()
    nwc = P.sbuf("nwc", [128, NG], F32)
    nbc = P.sbuf("nbc", [128, NG], F32)
    vbias_col = P.sbuf("vbias_col", [128, NG], F32)
    C.load(nw, nwc[:, :])
    C.load(nb, nbc[:, :])
    C.load(vb, vbias_col[:, :])
    qkvb_r = P.sbuf("qkvb_r", [1, 3 * D], BF16)
    C.load_cast(qkvb, qkvb_r[0:1, :])

    xt = P.sbuf("xt", [128, D], F32)
    ybf = P.sbuf("ybf", [128, D], BF16)
    sq = P.sbuf("sq", [128, D], F32)
    cols = P.sbuf("cols", [128, 8], F32)
    hT = P.sbuf("hT", [128, NG * TPC], BF16)
    q_sb = P.sbuf("q_sb", [128, NT * D], BF16)
    k_sb = P.sbuf("k_sb", [128, NT * D], BF16)
    bounce = P.sbuf("bounce", [128, 512], F32)
    ps_m = [P.psum(f"ps_m{j}", [128, 512], F32) for j in range(4)]

    for i in range(NT):
        C.load(x[i * 128:(i + 1) * 128, :], xt[:], wait=[C.sVE, C.sACT])
        C.ln_tile(xt[:], ybf[:], sq[:], cols)
        for g in range(NG):
            P.on("tensor", lambda e, g=g: e.transpose(
                pst[:, :], ybf[:, g * 128:(g + 1) * 128], C.identb[:]),
                 wait=[C.sVE], inc=C.sPE)
            P.on("vector", lambda e, i=i, g=g: e.scalar_tensor_tensor(
                hT[:, g * TPC + i * 128: g * TPC + (i + 1) * 128],
                pst[:, :], nwc[:, g:g + 1],
                bcast_col(nbc[:, g:g + 1], 128), ALU.mult, ALU.add),
                 wait=[C.sPE, C.sX], inc=C.sVE)
    for i in range(NT):
        for oh in range(2):
            pq, pk = ps_m[oh], ps_m[2 + oh]
            for ic in range(NG):
                P.on("tensor", lambda e, i=i, oh=oh, ic=ic, pq=pq: e.matmul(
                    pq[:, :], hT[:, ic * TPC + i * 128: ic * TPC + (i + 1) * 128],
                    qkvT[:, ic * 3 * D + oh * 512: ic * 3 * D + oh * 512 + 512],
                    start=(ic == 0), stop=False),
                     wait=[C.sW, C.sVE], inc=C.sPE)
            P.on("tensor", lambda e, oh=oh, pq=pq: e.matmul(
                pq[:, :], C.ones_row_b[0:1, :],
                qkvb_r[0:1, oh * 512: oh * 512 + 512],
                start=False, stop=True), wait=[C.sW], inc=C.sPE)
            for ic in range(NG):
                P.on("tensor", lambda e, i=i, oh=oh, ic=ic, pk=pk: e.matmul(
                    pk[:, :], hT[:, ic * TPC + i * 128: ic * TPC + (i + 1) * 128],
                    qkvT[:, ic * 3 * D + D + oh * 512:
                         ic * 3 * D + D + oh * 512 + 512],
                    start=(ic == 0), stop=False),
                     wait=[C.sW, C.sVE], inc=C.sPE)
            P.on("tensor", lambda e, oh=oh, pk=pk: e.matmul(
                pk[:, :], C.ones_row_b[0:1, :],
                qkvb_r[0:1, D + oh * 512: D + oh * 512 + 512],
                start=False, stop=True), wait=[C.sW], inc=C.sPE)
            P.on("vector", lambda e, i=i, oh=oh, pq=pq: e.tensor_copy(
                q_sb[:, i * D + oh * 512: i * D + oh * 512 + 512], pq[:, :]),
                 wait=[C.sPE], inc=C.sVE)
            P.on("vector", lambda e, i=i, oh=oh, pk=pk: e.tensor_scalar_mul(
                k_sb[:, i * D + oh * 512: i * D + oh * 512 + 512], pk[:, :],
                L2SCALE), wait=[C.sPE], inc=C.sVE)
    for qi in range(NG):
        for kh in range(2):
            pm = ps_m[kh]
            for i in range(NT):
                P.on("tensor", lambda e, qi=qi, kh=kh, i=i, pm=pm: e.matmul(
                    pm[:, :], q_sb[:, i * D + qi * 128: i * D + (qi + 1) * 128],
                    k_sb[:, i * D + kh * 512: i * D + kh * 512 + 512],
                    start=(i == 0), stop=(i == NT - 1)),
                     wait=[C.sVE], inc=C.sPE)
            P.on("vector", lambda e, pm=pm: e.tensor_copy(bounce[:, :],
                                                          pm[:, :]),
                 wait=[C.sPE, C.sOUT], inc=C.sVE)
            C.store(spart_o[qi * 128:(qi + 1) * 128,
                            kh * 512: kh * 512 + 512], bounce[:, :],
                    wait=[C.sVE])
    for og in range(NG):
        for tb in range(NTB):
            pm = ps_m[tb % 2]
            for ic in range(NG):
                P.on("tensor", lambda e, og=og, tb=tb, ic=ic, pm=pm: e.matmul(
                    pm[:, :],
                    qkvT[:, ic * 3 * D + 2 * D + og * 128:
                         ic * 3 * D + 2 * D + (og + 1) * 128],
                    hT[:, ic * TPC + tb * 512: ic * TPC + tb * 512 + 512],
                    start=(ic == 0), stop=(ic == NG - 1)),
                     wait=[C.sW, C.sVE], inc=C.sPE)
            P.on("vector", lambda e, og=og, tb=tb, pm=pm: e.tensor_scalar(
                k_sb[:, og * TPC + tb * 512: og * TPC + tb * 512 + 512],
                pm[:, :], vbias_col[:, og:og + 1], 0.0, ALU.add, ALU.add),
                 wait=[C.sPE, C.sVE], inc=C.sVE)
    C.store(vt_o.rearrange("(g p) t -> p g t", p=128),
            k_sb[:].rearrange("p (g t) -> p g t", g=NG), wait=[C.sVE])
    P.barrier_wait("gpsimd", C.sOUT)
    return P


def build_l2a_main():
    P = Prog()
    x = P.dram("x", [TPC, D], F32, kind="ExternalInput").ap()
    sparts = P.dram("Sparts", [4 * D, D], F32, kind="ExternalInput").ap()
    vt_i = P.dram("vT", [D, TPC], BF16, kind="ExternalInput").ap()
    pw = P.dram("proj_w", [D, D], F32, kind="ExternalInput").ap()
    pb = P.dram("proj_b", [128, NG], F32, kind="ExternalInput").ap()
    xout = P.dram("xout", [TPC, D], F32, kind="ExternalOutput").ap()
    C = Common(P)
    projT = P.sbuf("projT", [128, NG * D], BF16)
    stage = P.sbuf("stage", [128, D], BF16)
    pst = P.psum("pst", [128, 128], BF16)
    C.load_w_transposed(pw, projT, D, D, stage, pst)
    pbc = P.sbuf("pbc", [128, NG], F32)
    C.load(pb, pbc[:, :])
    S_sb = P.sbuf("S_sb", [128, NG * D], F32)
    S_tmp = P.sbuf("S_tmp", [128, NG * D], F32)
    vT = P.sbuf("vT_t", [128, NG * TPC], BF16)
    C.load(vt_i.rearrange("(g p) t -> p g t", p=128),
           vT[:].rearrange("p (g t) -> p g t", g=NG))
    aT = S_sb[:].bitcast(BF16)
    wmat = S_tmp[:].bitcast(BF16)
    cols = P.sbuf("cols", [128, 8], F32)
    xt = P.sbuf("xt", [128, D], F32)
    yf32 = P.sbuf("yf32", [128, D], F32)
    sq = P.sbuf("sq", [128, D], F32)
    fT = P.sbuf("fT", [128, NG * 512], F32)
    ps_m = [P.psum(f"ps_m{j}", [128, 512], F32) for j in range(4)]
    pstf = P.psum("pstf", [128, 128], F32)

    C.load(sparts[0:D, :].rearrange("(qi p) k -> p qi k", p=128),
           S_sb[:].rearrange("p (qi k) -> p qi k", qi=NG))
    for part in range(1, 4):
        C.load(sparts[part * D:(part + 1) * D, :].rearrange(
            "(qi p) k -> p qi k", p=128),
            S_tmp[:].rearrange("p (qi k) -> p qi k", qi=NG),
            wait=[C.sVE])
        P.on("vector", lambda e: e.tensor_tensor(
            S_sb[:, :], S_sb[:, :], S_tmp[:, :], ALU.add),
             wait=[C.sX, C.sVE], inc=C.sVE)
    for qi in range(NG):
        sl = S_sb[:, qi * D:(qi + 1) * D]
        P.on("vector", lambda e, sl=sl: e.tensor_reduce(
            cols[:, 0:1], sl, AX.X, ALU.max, negate=True),
             wait=[C.sVE], inc=C.sVE)
        P.on("scalar", lambda e, sl=sl: e.activation(
            sl, sl, AF.Exp, bias=cols[:, 0:1]),
             wait=[C.sVE], inc=C.sACT)
        P.on("vector", lambda e, sl=sl: e.tensor_reduce(
            cols[:, 1:2], sl, AX.X, ALU.add), wait=[C.sACT], inc=C.sVE)
        P.on("vector", lambda e: e.reciprocal(cols[:, 2:3], cols[:, 1:2]),
             wait=[C.sVE], inc=C.sVE)
        P.on("vector", lambda e, sl=sl, qi=qi: e.tensor_scalar(
            wmat[:, qi * D:(qi + 1) * D], sl, cols[:, 2:3], 0.0,
            ALU.mult, ALU.add), wait=[C.sVE], inc=C.sVE)
    for qi in range(NG):
        for kc in range(NG):
            P.on("tensor", lambda e, qi=qi, kc=kc: e.transpose(
                pst[:, :], wmat[:, qi * D + kc * 128: qi * D + (kc + 1) * 128],
                C.identb[:]), wait=[C.sVE], inc=C.sPE)
            P.on("vector", lambda e, qi=qi, kc=kc: e.tensor_copy(
                wmat[:, 8192 + kc * D + qi * 128: 8192 + kc * D + (qi + 1) * 128],
                pst[:, :]), wait=[C.sPE], inc=C.sVE)
    for qg in range(NG):
        for tb in range(NTB):
            pm = ps_m[tb % 2]
            for kc in range(NG):
                P.on("tensor", lambda e, qg=qg, tb=tb, kc=kc, pm=pm: e.matmul(
                    pm[:, :],
                    wmat[:, 8192 + kc * D + qg * 128:
                         8192 + kc * D + (qg + 1) * 128],
                    vT[:, kc * TPC + tb * 512: kc * TPC + tb * 512 + 512],
                    start=(kc == 0), stop=(kc == NG - 1)),
                     wait=[C.sVE, C.sX], inc=C.sPE)
            P.on("vector", lambda e, qg=qg, tb=tb, pm=pm: e.tensor_copy(
                aT[:, qg * TPC + tb * 512: qg * TPC + tb * 512 + 512],
                pm[:, :]), wait=[C.sPE, C.sVE], inc=C.sVE)
    for tb in range(NTB):
        for og in range(NG):
            pm = ps_m[og % 2]
            for ic in range(NG):
                P.on("tensor", lambda e, og=og, tb=tb, ic=ic, pm=pm: e.matmul(
                    pm[:, :], projT[:, ic * D + og * 128: ic * D + (og + 1) * 128],
                    aT[:, ic * TPC + tb * 512: ic * TPC + tb * 512 + 512],
                    start=(ic == 0), stop=(ic == NG - 1)),
                     wait=[C.sW, C.sVE], inc=C.sPE)
            P.on("vector", lambda e, og=og, pm=pm: e.tensor_scalar(
                fT[:, og * 512:(og + 1) * 512], pm[:, :], pbc[:, og:og + 1],
                0.0, ALU.add, ALU.add), wait=[C.sPE, C.sX, C.sVE],
                 inc=C.sVE)
        for ii in range(4):
            i = tb * 4 + ii
            C.load(x[i * 128:(i + 1) * 128, :], xt[:],
                   wait=[C.sVE, C.sOUT, C.sACT])
            C.ln_tile(xt[:], yf32[:], sq[:], cols)
            P.on("vector", lambda e: e.tensor_tensor(
                xt[:, :], xt[:, :], yf32[:, :], ALU.add),
                 wait=[C.sVE], inc=C.sVE)
            for g in range(NG):
                P.on("tensor", lambda e, ii=ii, g=g: e.transpose(
                    pstf[:, :],
                    fT[:, g * 512 + ii * 128: g * 512 + (ii + 1) * 128],
                    C.identf[:]), wait=[C.sVE], inc=C.sPE)
                P.on("vector", lambda e, g=g: e.scalar_tensor_tensor(
                    xt[:, g * 128:(g + 1) * 128], pstf[:, :], 1.0,
                    xt[:, g * 128:(g + 1) * 128], ALU.mult, ALU.add),
                     wait=[C.sPE], inc=C.sVE)
            C.store(xout[i * 128:(i + 1) * 128, :], xt[:], wait=[C.sVE])
    P.barrier_wait("gpsimd", C.sOUT)
    return P


def l2a_layer(xs, w, pfx):
    in1 = []
    for c in range(NCORES):
        in1.append({"x": xs[c], "qkv_w": w[f"{pfx}_qkv_w"],
                    "qkv_b": w[f"{pfx}_qkv_b"].reshape(1, -1),
                    "vb": cm(w[f"{pfx}_qkv_b"][2 * D:]),
                    "norm_w": cm(w[f"{pfx}_norm_w"]),
                    "norm_b": cm(w[f"{pfx}_norm_b"])})
    r = _run(build_l2a_pre, in1, "l2a_pre")
    sparts = [r.results[c]["Spart"] for c in range(NCORES)]
    vts = [r.results[c]["vT"] for c in range(NCORES)]
    in2 = []
    for c in range(NCORES):
        b0 = (c // 4) * 4
        sp = np.ascontiguousarray(np.concatenate(
            [sparts[b0 + j] for j in range(4)], axis=0))
        in2.append({"x": xs[c], "Sparts": sp, "vT": vts[c],
                    "proj_w": w[f"{pfx}_proj_w"],
                    "proj_b": cm(w[f"{pfx}_proj_b"])})
    r = _run(build_l2a_main, in2, "l2a_main")
    return [r.results[c]["xout"] for c in range(NCORES)]




TWO_PI = 6.283185307179586


def emit_rope_tables(P, C, costab, sintab, poscol, invf_bc, rtmp, irnd):
    ang, m_, s_, rt_ = (rtmp[:, 0:32], rtmp[:, 32:64], rtmp[:, 64:96],
                        rtmp[:, 96:128])
    P.on("vector", lambda e: e.tensor_scalar(
        ang[:, :], invf_bc[:, :], poscol, 0.0, ALU.mult, ALU.add),
         wait=[C.sVE, C.sPL, C.sACT], inc=C.sVE)
    P.on("vector", lambda e: e.tensor_scalar(
        m_[:, :], ang[:, :], 1.0 / TWO_PI, 0.0, ALU.mult, ALU.add),
         wait=[C.sVE], inc=C.sVE)
    P.on("vector", lambda e: e.tensor_copy(irnd[:, :], m_[:, :]),
         wait=[C.sVE], inc=C.sVE)
    P.on("vector", lambda e: e.tensor_copy(s_[:, :], irnd[:, :]),
         wait=[C.sVE], inc=C.sVE)
    P.on("vector", lambda e: e.tensor_tensor(m_[:, :], m_[:, :], s_[:, :],
                                             ALU.subtract),
         wait=[C.sVE], inc=C.sVE)
    P.on("scalar", lambda e: e.activation(s_[:, :], m_[:, :], AF.Sin,
                                          scale=TWO_PI / 2.0),
         wait=[C.sVE], inc=C.sACT)
    P.on("vector", lambda e: e.tensor_tensor(rt_[:, :], s_[:, :], s_[:, :],
                                             ALU.mult),
         wait=[C.sACT, C.sVE], inc=C.sVE)
    P.on("vector", lambda e: e.tensor_scalar(
        costab, rt_[:, :], -2.0, 1.0, ALU.mult, ALU.add),
         wait=[C.sVE], inc=C.sVE)
    P.on("vector", lambda e: e.tensor_scalar(
        rt_[:, :], rt_[:, :], -1.0, 1.0, ALU.mult, ALU.add),
         wait=[C.sVE], inc=C.sVE)
    P.on("scalar", lambda e: e.activation(rt_[:, :], rt_[:, :], AF.Sqrt),
         wait=[C.sVE], inc=C.sACT)
    P.on("vector", lambda e: e.tensor_tensor(sintab, s_[:, :], rt_[:, :],
                                             ALU.mult),
         wait=[C.sACT, C.sVE], inc=C.sVE)
    P.on("vector", lambda e: e.tensor_scalar_mul(sintab, sintab, 2.0),
         wait=[C.sVE], inc=C.sVE)


def build_l1a_pre():
    P = Prog()
    x = P.dram("x", [TPC, D], F32, kind="ExternalInput").ap()
    wq = P.dram("Wq", [D, D], F32, kind="ExternalInput").ap()
    wk = P.dram("Wk", [D, D], F32, kind="ExternalInput").ap()
    wv = P.dram("Wv", [D, D], F32, kind="ExternalInput").ap()
    bq = P.dram("bq", [1, D], F32, kind="ExternalInput").ap()
    bk = P.dram("bk", [1, D], F32, kind="ExternalInput").ap()
    bv = P.dram("bv", [1, D], F32, kind="ExternalInput").ap()
    qoff_i = P.dram("qoff", [1, 1], F32, kind="ExternalInput").ap()
    qt_o = P.dram("qT", [D, TPC], BF16, kind="ExternalOutput").ap()
    kt_o = P.dram("kT", [D, TPC], BF16, kind="ExternalOutput").ap()
    va_o = P.dram("Va", [TPC, 16 * 65], BF16, kind="ExternalOutput").ap()
    qn_o = P.dram("qn2", [TPC, 16], F32, kind="ExternalOutput").ap()
    kn_o = P.dram("knmax", [16, 1], F32, kind="ExternalOutput").ap()
    C = Common(P)
    NH_ = 16
    wqT = P.sbuf("wqT", [128, NG * D], BF16)
    wkT = P.sbuf("wkT", [128, NG * D], BF16)
    wvT = P.sbuf("wvT", [128, NG * D], BF16)
    stage = P.sbuf("stage", [128, D], BF16)
    pst = P.psum("pst", [128, 128], BF16)
    C.load_w_transposed(wq, wqT, D, D, stage, pst)
    C.load_w_transposed(wk, wkT, D, D, stage, pst)
    C.load_w_transposed(wv, wvT, D, D, stage, pst)
    bqr = P.sbuf("bqr", [1, D], BF16)
    bkr = P.sbuf("bkr", [1, D], BF16)
    bvr = P.sbuf("bvr", [1, D], BF16)
    C.load_cast(bq, bqr[0:1, :])
    C.load_cast(bk, bkr[0:1, :])
    C.load_cast(bv, bvr[0:1, :])
    qoff = P.sbuf("qoff_t", [1, 1], F32)
    C.load(qoff_i, qoff[0:1, :])
    qoffc = P.sbuf("qoffc", [128, 1], F32)
    ones_row_f1 = P.sbuf("ones_row_f1", [1, 128], F32)
    P.on("gpsimd", lambda e: e.memset(ones_row_f1[:], 1.0), inc=C.sPL)

    xt = P.sbuf("xt", [128, D], F32)
    ybf = P.sbuf("ybf", [128, D], BF16)
    sq = P.sbuf("sq", [128, D], F32)
    cols = P.sbuf("cols", [128, 8], F32)
    yT = P.sbuf("yT", [128, NG * TPC], BF16)
    qtm = P.sbuf("qtm", [128, D], F32)
    ktm = qtm
    qro = P.sbuf("qro", [128, D], BF16)
    kro = qro
    qT_sb = P.sbuf("qT_sb", [128, NG * TPC], BF16)
    kT_sb = P.sbuf("kT_sb", [128, NG * TPC], BF16)
    va_sb = P.sbuf("va_sb", [128, NT * 1040], BF16)
    qn_sb = P.sbuf("qn_sb", [128, NT * 16], F32)
    knm = P.sbuf("knm", [16, 1], F32)
    kn_t = P.sbuf("kn_t", [128, 16], F32)
    costab = P.sbuf("costab", [128, 32], F32)
    sintab = P.sbuf("sintab", [128, 32], F32)
    invf_bc = P.sbuf("invf_bc", [128, 32], F32)
    invf_r = P.sbuf("invf_r", [1, 32], F32)
    rtmp = P.sbuf("rtmp", [128, 128], F32)
    irnd = P.sbuf("irnd", [128, 32], I32)
    poscol = P.sbuf("poscol", [128, 2], F32)
    ps_m = [P.psum(f"ps_m{j}", [128, 512], F32) for j in range(4)]
    ps_n = P.psum("ps_n", [16, 128], F32)
    pstf = ps_m[3][:, 0:128]

    import math as _math
    P.on("gpsimd", lambda e: e.iota(rtmp[0:1, 0:32], pattern=[[1, 32]],
                                    base=0, channel_multiplier=0,
                                    allow_small_or_imprecise_dtypes=True),
         inc=C.sPL)
    P.on("scalar", lambda e: e.activation(
        rtmp[0:1, 32:64], rtmp[0:1, 0:32], AF.Exp,
        scale=-_math.log(10000.0) / 32.0), wait=[C.sPL], inc=C.sACT)
    P.on("vector", lambda e: e.tensor_copy(invf_r[0:1, :],
                                           rtmp[0:1, 32:64]),
         wait=[C.sACT], inc=C.sVE)
    P.on("tensor", lambda e: e.matmul(
        pstf[:, 0:32], ones_row_f1[0:1, :], invf_r[0:1, :],
        start=True, stop=True), wait=[C.sVE, C.sPL], inc=C.sPE)
    P.on("vector", lambda e: e.tensor_copy(invf_bc[:, :], pstf[:, 0:32]),
         wait=[C.sPE], inc=C.sVE)
    P.on("gpsimd", lambda e: e.iota(poscol[:, 0:1], pattern=[[0, 1]],
                                    base=0, channel_multiplier=1,
                                    allow_small_or_imprecise_dtypes=True),
         inc=C.sPL)
    P.on("tensor", lambda e: e.matmul(
        pstf[:, 0:1], ones_row_f1[0:1, :], qoff[0:1, 0:1],
        start=True, stop=True), wait=[C.sPL, C.sX, C.sVE], inc=C.sPE)
    P.on("vector", lambda e: e.tensor_copy(qoffc[:, :], pstf[:, 0:1]),
         wait=[C.sPE], inc=C.sVE)

    def stt_rope(dst, a1, a2, tab1, sgn, tab2, tmp):
        P.on("vector", lambda e: e.tensor_tensor(tmp, a2, tab2, ALU.mult),
             wait=[C.sVE, C.sACT], inc=C.sVE)
        P.on("vector", lambda e: e.tensor_tensor(dst, a1, tab1, ALU.mult),
             wait=[C.sVE], inc=C.sVE)
        P.on("vector", lambda e: e.scalar_tensor_tensor(
            dst, tmp, sgn, dst, ALU.mult, ALU.add),
             wait=[C.sVE], inc=C.sVE)

    for i in range(NT):
        C.load(x[i * 128:(i + 1) * 128, :], xt[:], wait=[C.sVE, C.sACT])
        C.ln_tile(xt[:], ybf[:], sq[:], cols)
        for g in range(NG):
            P.on("tensor", lambda e, g=g: e.transpose(
                pst[:, :], ybf[:, g * 128:(g + 1) * 128], C.identb[:]),
                 wait=[C.sVE], inc=C.sPE)
            P.on("vector", lambda e, i=i, g=g: e.tensor_copy(
                yT[:, g * TPC + i * 128: g * TPC + (i + 1) * 128],
                pst[:, :]), wait=[C.sPE], inc=C.sVE)
        P.on("vector", lambda e, i=i: e.tensor_scalar(
            poscol[:, 1:2], poscol[:, 0:1], 1.0, float(i * 128),
            ALU.mult, ALU.add), wait=[C.sVE, C.sPL, C.sX], inc=C.sVE)
        P.on("vector", lambda e: e.tensor_tensor(
            poscol[:, 1:2], poscol[:, 1:2], qoffc[:, :], ALU.add),
             wait=[C.sVE, C.sX], inc=C.sVE)
        emit_rope_tables(P, C, costab[:, :], sintab[:, :],
                         poscol[:, 1:2], invf_bc, rtmp, irnd)
        for (wT, br, tm, ro, dT, isq) in (
                (wqT, bqr, qtm, qro, qT_sb, True),
                (wkT, bkr, ktm, kro, kT_sb, False)):
            for oh in range(2):
                pm = ps_m[oh]
                for ic in range(NG):
                    P.on("tensor", lambda e, i=i, oh=oh, ic=ic, pm=pm,
                         wT=wT: e.matmul(
                         pm[:, :],
                         yT[:, ic * TPC + i * 128: ic * TPC + (i + 1) * 128],
                         wT[:, ic * D + oh * 512: ic * D + oh * 512 + 512],
                         start=(ic == 0), stop=False),
                         wait=[C.sW, C.sVE], inc=C.sPE)
                P.on("tensor", lambda e, oh=oh, pm=pm, br=br: e.matmul(
                    pm[:, :], C.ones_row_b[0:1, :],
                    br[0:1, oh * 512: oh * 512 + 512],
                    start=False, stop=True), wait=[C.sW], inc=C.sPE)
                P.on("vector", lambda e, oh=oh, pm=pm, tm=tm: e.tensor_copy(
                    tm[:, oh * 512: oh * 512 + 512], pm[:, :]),
                     wait=[C.sPE, C.sVE, C.sACT], inc=C.sVE)
            for h in range(NH_):
                x1 = tm[:, h * 64: h * 64 + 32]
                x2 = tm[:, h * 64 + 32: h * 64 + 64]
                o1 = ro[:, h * 64: h * 64 + 32]
                o2 = ro[:, h * 64 + 32: h * 64 + 64]
                stt_rope(o1, x1, x2, costab[:, :], -1.0, sintab[:, :],
                         rtmp[:, 96:128])
                stt_rope(o2, x2, x1, costab[:, :], 1.0, sintab[:, :],
                         rtmp[:, 96:128])
            P.on("vector", lambda e, ro=ro: e.tensor_tensor(
                sq[:, :], ro[:, :], ro[:, :], ALU.mult),
                 wait=[C.sVE, C.sACT], inc=C.sVE)
            if isq:
                for h in range(NH_):
                    P.on("vector", lambda e, i=i, h=h: e.tensor_reduce(
                        qn_sb[:, i * 16 + h: i * 16 + h + 1],
                        sq[:, h * 64:(h + 1) * 64], AX.X, ALU.add),
                         wait=[C.sVE], inc=C.sVE)
            else:
                for h in range(NH_):
                    P.on("vector", lambda e, h=h: e.tensor_reduce(
                        kn_t[:, h:h + 1], sq[:, h * 64:(h + 1) * 64],
                        AX.X, ALU.add), wait=[C.sVE], inc=C.sVE)
                P.on("tensor", lambda e: e.transpose(
                    ps_n[0:16, 0:128], kn_t[:, :], C.identf[:]),
                     wait=[C.sVE], inc=C.sPE)
                P.on("vector", lambda e: e.tensor_reduce(
                    kn_t[0:16, 0:1], ps_n[0:16, :], AX.X, ALU.max),
                     wait=[C.sPE, C.sVE], inc=C.sVE)
                if i == 0:
                    P.on("vector", lambda e: e.tensor_copy(
                        knm[0:16, :], kn_t[0:16, 0:1]),
                         wait=[C.sVE], inc=C.sVE)
                else:
                    P.on("vector", lambda e: e.tensor_tensor(
                        knm[0:16, :], knm[0:16, :], kn_t[0:16, 0:1],
                        ALU.max), wait=[C.sVE], inc=C.sVE)
            for g in range(NG):
                P.on("tensor", lambda e, ro=ro, g=g: e.transpose(
                    pst[:, :], ro[:, g * 128:(g + 1) * 128], C.identb[:]),
                     wait=[C.sVE], inc=C.sPE)
                P.on("vector", lambda e, i=i, g=g, dT=dT: e.tensor_copy(
                    dT[:, g * TPC + i * 128: g * TPC + (i + 1) * 128],
                    pst[:, :]), wait=[C.sPE], inc=C.sVE)
        for oh in range(2):
            pm = ps_m[2 + oh]
            for ic in range(NG):
                P.on("tensor", lambda e, i=i, oh=oh, ic=ic, pm=pm: e.matmul(
                    pm[:, :],
                    yT[:, ic * TPC + i * 128: ic * TPC + (i + 1) * 128],
                    wvT[:, ic * D + oh * 512: ic * D + oh * 512 + 512],
                    start=(ic == 0), stop=False),
                     wait=[C.sW, C.sVE], inc=C.sPE)
            P.on("tensor", lambda e, oh=oh, pm=pm: e.matmul(
                pm[:, :], C.ones_row_b[0:1, :],
                bvr[0:1, oh * 512: oh * 512 + 512],
                start=False, stop=True), wait=[C.sW], inc=C.sPE)
            for hh in range(8):
                h = oh * 8 + hh
                P.on("vector", lambda e, i=i, h=h, hh=hh, pm=pm:
                     e.tensor_copy(
                     va_sb[:, i * 1040 + h * 65: i * 1040 + h * 65 + 64],
                     pm[:, hh * 64:(hh + 1) * 64]),
                     wait=[C.sPE, C.sVE], inc=C.sVE)
    P.on("gpsimd", lambda e: e.memset(
        va_sb[:].rearrange("p (i h o) -> p i h o", i=NT, h=16)
        [:, :, :, 64:65], 1.0), wait=[C.sVE], inc=C.sPL)
    C.store(qt_o.rearrange("(g p) t -> p g t", p=128),
            qT_sb[:].rearrange("p (g t) -> p g t", g=NG), wait=[C.sVE])
    C.store(kt_o.rearrange("(g p) t -> p g t", p=128),
            kT_sb[:].rearrange("p (g t) -> p g t", g=NG), wait=[C.sVE])
    C.store(va_o.rearrange("(i p) c -> p i c", p=128),
            va_sb[:].rearrange("p (i c) -> p i c", i=NT),
            wait=[C.sVE, C.sPL])
    C.store(qn_o.rearrange("(i p) h -> p i h", p=128),
            qn_sb[:].rearrange("p (i h) -> p i h", i=NT), wait=[C.sVE])
    C.store(kn_o, knm[0:16, 0:1], wait=[C.sVE])
    P.barrier_wait("gpsimd", C.sOUT)
    return P




def build_l1a_main(kvlen):
    P = Prog()
    NH_ = 16
    SCALE = 0.125
    x = P.dram("x", [TPC, D], F32, kind="ExternalInput").ap()
    qt_i = P.dram("qT", [D, TPC], BF16, kind="ExternalInput").ap()
    kt_i = P.dram("kTall", [D, kvlen], BF16, kind="ExternalInput").ap()
    va_i = P.dram("Vall", [kvlen, 1040], BF16, kind="ExternalInput").ap()
    qn_i = P.dram("qn2", [TPC, NH_], F32, kind="ExternalInput").ap()
    kn_i = P.dram("knmax", [4 * 16, 1], F32, kind="ExternalInput").ap()
    wo = P.dram("Wo", [D, D], F32, kind="ExternalInput").ap()
    bo = P.dram("bo", [128, NG], F32, kind="ExternalInput").ap()
    qoff_i = P.dram("qoff", [1, 1], F32, kind="ExternalInput").ap()
    xout = P.dram("xout", [TPC, D], F32, kind="ExternalOutput").ap()
    C = Common(P)
    woT = P.sbuf("woT", [128, NG * D], BF16)
    stage = P.sbuf("stage", [128, D], BF16)
    pst = P.psum("pst", [128, 128], BF16)
    C.load_w_transposed(wo, woT, D, D, stage, pst)
    boc = P.sbuf("boc", [128, NG], F32)
    C.load(bo, boc[:, :])
    ps8 = [P.psum(f"ps8_{j}", [128, 512], F32) for j in range(6)]
    pstf = ps8[2][:, 0:128]
    ps_n = ps8[3][0:16, 0:128]
    qT = P.sbuf("qT_t", [128, NG * TPC], BF16)
    C.load(qt_i.rearrange("(g p) t -> p g t", p=128),
           qT[:].rearrange("p (g t) -> p g t", g=NG))
    qn = P.sbuf("qn_t", [128, NT * NH_], F32)
    C.load(qn_i.rearrange("(i p) h -> p i h", p=128),
           qn[:].rearrange("p (i h) -> p i h", i=NT))
    knv = P.sbuf("knv", [64, 1], F32)
    C.load(kn_i, knv[0:64, :])
    ones_row_f = P.sbuf("ones_row_f", [1, 128], F32)
    onesall = P.sbuf("onesall", [128, 64], F32)
    tri01 = P.sbuf("tri01", [128, 128], BF16)

    def setup(g):
        g.memset(ones_row_f[:], 1.0)
        g.memset(onesall[:], 1.0)
        g.memset(tri01[:], 1.0)
        return g.affine_select(out=tri01[:], in_=tri01[:],
                               compare_op=ALU.is_ge, fill=0.0, base=0,
                               pattern=[[1, 128]], channel_multiplier=-1)
    P.on("gpsimd", setup, inc=C.sPL)

    knm = P.sbuf("knm", [16, 8], F32)
    for cc in range(4):
        P.on("gpsimd", lambda e, cc=cc: e.dma_start(
            out=knm[0:16, 4 + cc:5 + cc],
            in_=knv[cc * 16:(cc + 1) * 16, 0:1]),
             wait=[C.sX], inc=(C.sX, 16))
    P.on("vector", lambda e: e.tensor_reduce(
        knm[0:16, 0:1], knm[0:16, 4:8], AX.X, ALU.max),
         wait=[C.sX], inc=C.sVE)
    P.on("scalar", lambda e: e.activation(knm[0:16, 1:2], knm[0:16, 0:1],
                                          AF.Sqrt), wait=[C.sVE],
         inc=C.sACT)
    nbT = P.sbuf("nbT", [16, TPC], BF16)
    for i in range(NT):
        P.on("tensor", lambda e, i=i: e.transpose(
            ps_n[0:16, 0:128], qn[:, i * NH_:(i + 1) * NH_], C.identf[:]),
             wait=[C.sVE, C.sX, C.sPL, C.sACT], inc=C.sPE)
        P.on("scalar", lambda e, i=i: e.activation(
            nbT[0:16, i * 128:(i + 1) * 128], ps_n[0:16, :], AF.Sqrt),
             wait=[C.sPE, C.sACT], inc=C.sACT)
    P.on("vector", lambda e: e.tensor_scalar(
        nbT[0:16, :], nbT[0:16, :], knm[0:16, 1:2], -1.0,
        ALU.mult, ALU.mult), wait=[C.sACT, C.sVE], inc=C.sVE)

    ktc = [P.sbuf(f"ktc{j}", [128, NG * 512], BF16) for j in range(2)]
    vac = [P.sbuf(f"vac{j}", [128, 4 * 1040], BF16) for j in range(2)]
    ptk = P.sbuf("ptk", [128, 4 * 512], BF16)
    brow = P.sbuf("brow", [1, 512], F32)
    accS = P.sbuf("accS", [128, 9 * 512], F32)
    accS2 = P.sbuf("accS2", [128, 7 * 512], F32)
    rlb = P.sbuf("rlb", [64, 512], F32)
    mrg = P.sbuf("mrg", [128, NG * TPC], BF16)
    fT = P.sbuf("fT", [128, NG * 512], F32)
    xt = P.sbuf("xt", [128, D], F32)

    def accS_ap(h):
        if h < 9:
            return accS[:, h * 512:(h + 1) * 512]
        return accS2[:, (h - 9) * 512:(h - 8) * 512]

    nkc = kvlen // 512
    browq = P.sbuf("browq", [1, NH_ * 512], BF16)
    for qb in range(NTB):
        myq0 = qb * 512
        kclim = (kvlen - TPC + (qb + 1) * 512) // 512
        for h in range(NH_):
            P.on("gpsimd", lambda e, h=h, qb=qb: e.dma_start(
                out=browq[0:1, h * 512:(h + 1) * 512],
                in_=nbT[h:h + 1, qb * 512:(qb + 1) * 512]),
                 wait=[C.sVE, C.sPE], inc=(C.sX, 16))
        for kc in range(kclim):
            diag = (kc == kclim - 1)
            sb = kc % 2
            C.load_cast(
                kt_i[:, kc * 512:(kc + 1) * 512].rearrange(
                    "(g p) t -> p g t", p=128),
                ktc[sb][:].rearrange("p (g t) -> p g t", g=NG),
                wait=[C.sPE], eng="sync")
            C.load_cast(
                va_i[kc * 512:(kc + 1) * 512, :].rearrange(
                    "(s p) c -> p s c", p=128),
                vac[sb][:].rearrange("p (s c) -> p s c", s=4),
                wait=[C.sPE], eng="gpsimd")
            for h in range(NH_):
                hrow = 64 * (h % 2)
                hg = h // 2
                for ks in range(4):
                    P.on("tensor", lambda e, h=h, qb=qb, kc=kc, ks=ks,
                         sb=sb, hrow=hrow, hg=hg: e.matmul(
                         ps8[ks][:, :],
                         ktc[sb][hrow:hrow + 64,
                                 hg * 512 + ks * 128:
                                 hg * 512 + (ks + 1) * 128],
                         qT[hrow:hrow + 64,
                            hg * TPC + qb * 512: hg * TPC + (qb + 1) * 512],
                         start=True, stop=False),
                         wait=[C.sW, C.sX, C.sVE, C.sACT], inc=C.sPE)
                    P.on("tensor", lambda e, ks=ks, h=h: e.matmul(
                        ps8[ks][:, :], C.ones_row_b[0:1, :],
                        browq[0:1, h * 512:(h + 1) * 512],
                        start=False, stop=True),
                         wait=[C.sX], inc=C.sPE)
                    P.on("scalar", lambda e, ks=ks: e.activation(
                        ptk[:, ks * 512:(ks + 1) * 512], ps8[ks][:, :],
                        AF.Exp, scale=SCALE),
                         wait=[C.sPE, C.sVE], inc=C.sACT)
                if diag:
                    for ks in range(4):
                        krel = ks * 128
                        if krel > 0:
                            P.on("vector", lambda e, ks=ks, krel=krel:
                                 e.tensor_scalar_mul(
                                 ptk[:, ks * 512: ks * 512 + krel],
                                 ptk[:, ks * 512: ks * 512 + krel], 0.0),
                                 wait=[C.sACT, C.sVE], inc=C.sVE)
                        P.on("vector", lambda e, ks=ks, krel=krel:
                             e.tensor_tensor(
                             ptk[:, ks * 512 + krel: ks * 512 + krel + 128],
                             ptk[:, ks * 512 + krel: ks * 512 + krel + 128],
                             tri01[:, :], ALU.mult),
                             wait=[C.sACT, C.sVE, C.sPL], inc=C.sVE)
                for ks in range(4):
                    P.on("tensor", lambda e, h=h, kc=kc, ks=ks, sb=sb:
                         e.matmul(
                         ps8[4][0:65, :],
                         vac[sb][:, ks * 1040 + h * 65:
                                 ks * 1040 + (h + 1) * 65],
                         ptk[:, ks * 512:(ks + 1) * 512],
                         start=(ks == 0), stop=(ks == 3)),
                         wait=[C.sACT, C.sVE], inc=C.sPE)
                if kc == 0:
                    P.on("vector", lambda e, h=h: e.tensor_copy(
                        accS_ap(h)[0:65, :], ps8[4][0:65, :]),
                         wait=[C.sPE, C.sVE], inc=C.sVE)
                else:
                    P.on("vector", lambda e, h=h: e.tensor_tensor(
                        accS_ap(h)[0:65, :], accS_ap(h)[0:65, :],
                        ps8[4][0:65, :], ALU.add),
                         wait=[C.sPE, C.sVE], inc=C.sVE)
        for h in range(NH_):
            P.on("vector", lambda e, h=h: e.reciprocal(
                accS_ap(h)[64:65, :], accS_ap(h)[64:65, :]),
                 wait=[C.sVE], inc=C.sVE)
            P.on("tensor", lambda e, h=h: e.matmul(
                ps8[5][0:64, :], onesall[64:65, 0:64],
                accS_ap(h)[64:65, :], start=True, stop=True),
                 wait=[C.sVE, C.sPL], inc=C.sPE)
            P.on("vector", lambda e, h=h: e.tensor_copy(
                rlb[0:64, :], ps8[5][0:64, :]), wait=[C.sPE, C.sVE],
                 inc=C.sVE)
            hrow = 64 * (h % 2)
            hg = h // 2
            P.on("vector", lambda e, h=h, hrow=hrow, hg=hg, qb=qb:
                 e.tensor_tensor(
                 mrg[hrow:hrow + 64,
                     hg * TPC + qb * 512: hg * TPC + (qb + 1) * 512],
                 accS_ap(h)[0:64, :], rlb[0:64, :], ALU.mult),
                 wait=[C.sVE], inc=C.sVE)
    for tb in range(NTB):
        for og in range(NG):
            pm = ps8[og % 2]
            for ic in range(NG):
                P.on("tensor", lambda e, og=og, tb=tb, ic=ic, pm=pm: e.matmul(
                    pm[:, :], woT[:, ic * D + og * 128: ic * D + (og + 1) * 128],
                    mrg[:, ic * TPC + tb * 512: ic * TPC + tb * 512 + 512],
                    start=(ic == 0), stop=(ic == NG - 1)),
                     wait=[C.sW, C.sVE], inc=C.sPE)
            P.on("vector", lambda e, og=og, pm=pm: e.tensor_scalar(
                fT[:, og * 512:(og + 1) * 512], pm[:, :], boc[:, og:og + 1],
                0.0, ALU.add, ALU.add), wait=[C.sPE, C.sX, C.sVE],
                 inc=C.sVE)
        for ii in range(4):
            i = tb * 4 + ii
            C.load(x[i * 128:(i + 1) * 128, :], xt[:],
                   wait=[C.sVE, C.sOUT])
            for g in range(NG):
                P.on("tensor", lambda e, ii=ii, g=g: e.transpose(
                    pstf[:, :],
                    fT[:, g * 512 + ii * 128: g * 512 + (ii + 1) * 128],
                    C.identf[:]), wait=[C.sVE], inc=C.sPE)
                P.on("vector", lambda e, g=g: e.scalar_tensor_tensor(
                    xt[:, g * 128:(g + 1) * 128], pstf[:, :], 1.0,
                    xt[:, g * 128:(g + 1) * 128], ALU.mult, ALU.add),
                     wait=[C.sPE, C.sX], inc=C.sVE)
            C.store(xout[i * 128:(i + 1) * 128, :], xt[:], wait=[C.sVE])
    P.barrier_wait("gpsimd", C.sOUT)
    return P


def l1a_layer(xs, w, pfx):
    in1 = []
    for c in range(NCORES):
        qoff = np.zeros((1, 1), np.float32)
        qoff[0, 0] = (c % 4) * TPC
        in1.append({"x": xs[c], "Wq": w[f"{pfx}_Wq"], "Wk": w[f"{pfx}_Wk"],
                    "Wv": w[f"{pfx}_Wv"],
                    "bq": w[f"{pfx}_bq"].reshape(1, -1),
                    "bk": w[f"{pfx}_bk"].reshape(1, -1),
                    "bv": w[f"{pfx}_bv"].reshape(1, -1),
                    "qoff": qoff})
    r = _run(build_l1a_pre, in1, "l1a_pre")
    qts = [r.results[c]["qT"] for c in range(NCORES)]
    kts = [r.results[c]["kT"] for c in range(NCORES)]
    vas = [r.results[c]["Va"] for c in range(NCORES)]
    qns = [r.results[c]["qn2"] for c in range(NCORES)]
    kns = [r.results[c]["knmax"] for c in range(NCORES)]
    in2 = [None] * NCORES
    for c in range(NCORES):
        b0 = (c // 4) * 4
        nq = c % 4 + 1
        kvlen = nq * TPC
        ktall = np.ascontiguousarray(np.concatenate(
            [kts[b0 + j] for j in range(nq)], axis=1))
        vall = np.ascontiguousarray(np.concatenate(
            [vas[b0 + j] for j in range(nq)], axis=0))
        knall = np.ascontiguousarray(np.concatenate(
            [kns[b0 + j] for j in range(4)], axis=0))
        qoff = np.zeros((1, 1), np.float32)
        qoff[0, 0] = (c % 4) * TPC
        in2[c] = {"x": xs[c], "qT": qts[c], "kTall": ktall, "Vall": vall,
                  "qn2": qns[c], "knmax": knall, "Wo": w[f"{pfx}_Wo"],
                  "bo": cm(w[f"{pfx}_bo"]), "qoff": qoff}
    outs = [None] * NCORES
    for qg in range(4):
        kvlen = (qg + 1) * TPC
        maps = [in2[qg], in2[qg + 4]]
        name = f"l1a_main{qg}"
        if name not in _CACHE:
            _CACHE[name] = build_l1a_main(kvlen).finalize()
        r = run_bass_kernel_spmd(_CACHE[name], maps, core_ids=[0, 1])
        outs[qg] = r.results[0]["xout"]
        outs[qg + 4] = r.results[1]["xout"]
    return outs




RMS_EPS = 1e-6


def emit_rms_scale(P, C, xt_f32, out_bf, cols, sq, wcol_tiles, pst, dT, i,
                   extra_col_scale=None):
    P.on("vector", lambda e: e.tensor_tensor(sq[:, :], xt_f32, xt_f32,
                                             ALU.mult),
         wait=[C.sVE, C.sACT, C.sX], inc=C.sVE)
    P.on("vector", lambda e: e.tensor_reduce(cols[:, 0:1], sq[:, :], AX.X,
                                             ALU.add),
         wait=[C.sVE], inc=C.sVE)
    P.on("vector", lambda e: e.tensor_scalar(cols[:, 1:2], cols[:, 0:1],
                                             1.0 / D, RMS_EPS, ALU.mult,
                                             ALU.add), wait=[C.sVE],
         inc=C.sVE)
    P.on("scalar", lambda e: e.activation(cols[:, 2:3], cols[:, 1:2],
                                          AF.Sqrt), wait=[C.sVE],
         inc=C.sACT)
    P.on("vector", lambda e: e.reciprocal(cols[:, 3:4], cols[:, 2:3]),
         wait=[C.sACT], inc=C.sVE)
    P.on("vector", lambda e: e.scalar_tensor_tensor(
        out_bf, xt_f32, 1.0, bcast_col(cols[:, 3:4], D), ALU.mult,
        ALU.mult), wait=[C.sVE], inc=C.sVE)
    for g in range(NG):
        P.on("tensor", lambda e, g=g: e.transpose(
            pst[:, :], out_bf[:, g * 128:(g + 1) * 128], C.identb[:]),
             wait=[C.sVE], inc=C.sPE)
        wc = wcol_tiles
        P.on("vector", lambda e, g=g, i=i: e.tensor_scalar(
            dT[:, g * TPC + i * 128: g * TPC + (i + 1) * 128], pst[:, :],
            wc[:, g:g + 1], 0.0, ALU.mult, ALU.add),
             wait=[C.sPE, C.sX], inc=C.sVE)


def emit_rope_il(P, C, tm, ro, costab, sintab, tmp32, nheads=16, hd=64):
    for h in range(nheads):
        base = h * hd
        xe = tm[:, base: base + hd].rearrange("p (j two) -> p j two", two=2)[:, :, 0:1]
        xo = tm[:, base: base + hd].rearrange("p (j two) -> p j two", two=2)[:, :, 1:2]
        oe = ro[:, base: base + hd].rearrange("p (j two) -> p j two", two=2)[:, :, 0:1]
        oo = ro[:, base: base + hd].rearrange("p (j two) -> p j two", two=2)[:, :, 1:2]
        ct = costab[:, :].rearrange("p (j one) -> p j one", one=1)
        st = sintab[:, :].rearrange("p (j one) -> p j one", one=1)
        t1 = tmp32[:, 0:32].rearrange("p (j one) -> p j one", one=1)
        P.on("vector", lambda e, xo=xo, st=st, t1=t1: e.tensor_tensor(
            t1, xo, st, ALU.mult), wait=[C.sVE, C.sACT], inc=C.sVE)
        P.on("vector", lambda e, oe=oe, xe=xe, ct=ct: e.tensor_tensor(
            oe, xe, ct, ALU.mult), wait=[C.sVE], inc=C.sVE)
        P.on("vector", lambda e, oe=oe, t1=t1: e.scalar_tensor_tensor(
            oe, t1, -1.0, oe, ALU.mult, ALU.add), wait=[C.sVE], inc=C.sVE)
        P.on("vector", lambda e, xe=xe, st=st, t1=t1: e.tensor_tensor(
            t1, xe, st, ALU.mult), wait=[C.sVE], inc=C.sVE)
        P.on("vector", lambda e, oo=oo, xo=xo, ct=ct: e.tensor_tensor(
            oo, xo, ct, ALU.mult), wait=[C.sVE], inc=C.sVE)
        P.on("vector", lambda e, oo=oo, t1=t1: e.tensor_tensor(
            oo, t1, oo, ALU.add), wait=[C.sVE], inc=C.sVE)


def emit_qknorm(P, C, ro, sq, cols, nheads, hd=64):
    P.on("vector", lambda e: e.tensor_tensor(sq[:, 0:nheads * hd], ro[:, :],
                                             ro[:, :], ALU.mult),
         wait=[C.sVE], inc=C.sVE)
    for h in range(nheads):
        P.on("vector", lambda e, h=h: e.tensor_reduce(
            cols[:, 4:5], sq[:, h * hd:(h + 1) * hd], AX.X, ALU.add),
             wait=[C.sVE], inc=C.sVE)
        P.on("vector", lambda e: e.tensor_scalar(
            cols[:, 5:6], cols[:, 4:5], 1.0 / hd, RMS_EPS, ALU.mult,
            ALU.add), wait=[C.sVE], inc=C.sVE)
        P.on("scalar", lambda e: e.activation(cols[:, 6:7], cols[:, 5:6],
                                              AF.Sqrt), wait=[C.sVE],
             inc=C.sACT)
        P.on("vector", lambda e: e.reciprocal(cols[:, 6:7], cols[:, 6:7]),
             wait=[C.sACT], inc=C.sVE)
        P.on("vector", lambda e, h=h: e.scalar_tensor_tensor(
            ro[:, h * hd:(h + 1) * hd], ro[:, h * hd:(h + 1) * hd],
            1.0, bcast_col(cols[:, 6:7], hd), ALU.mult, ALU.mult),
             wait=[C.sVE], inc=C.sVE)


def build_l3a_pre():
    P = Prog()
    x = P.dram("x", [TPC, D], F32, kind="ExternalInput").ap()
    gwq = P.dram("g_wq", [D, D], F32, kind="ExternalInput").ap()
    gkvd = P.dram("g_kv_down", [D, 256], F32, kind="ExternalInput").ap()
    gkup = P.dram("g_k_up", [256, 256], F32, kind="ExternalInput").ap()
    gvup = P.dram("g_v_up", [256, 256], F32, kind="ExternalInput").ap()
    n1 = P.dram("norm1_w", [128, NG], F32, kind="ExternalInput").ap()
    qoff_i = P.dram("qoff", [1, 1], F32, kind="ExternalInput").ap()
    qt_o = P.dram("qT", [D, TPC], BF16, kind="ExternalOutput").ap()
    kt_o = P.dram("kT", [256, TPC], BF16, kind="ExternalOutput").ap()
    va_o = P.dram("Va", [TPC, 4 * 65], BF16, kind="ExternalOutput").ap()
    C = Common(P)
    wqt = P.sbuf("wqt", [128, NG * D], BF16)
    kvdt = P.sbuf("kvdt", [128, NG * 256], BF16)
    kupt = P.sbuf("kupt", [128, 2 * 256], BF16)
    vupt = P.sbuf("vupt", [128, 2 * 256], BF16)
    C.load_w_direct(gwq, wqt, D, D)
    C.load_w_direct(gkvd, kvdt, D, 256)
    C.load_w_direct(gkup, kupt, 256, 256)
    C.load_w_direct(gvup, vupt, 256, 256)
    n1c = P.sbuf("n1c", [128, NG], F32)
    C.load(n1, n1c[:, :])
    qoff = P.sbuf("qoff_t", [1, 1], F32)
    C.load(qoff_i, qoff[0:1, :])
    ones_row_f1 = P.sbuf("ones_row_f1", [1, 128], F32)
    P.on("gpsimd", lambda e: e.memset(ones_row_f1[:], 1.0), inc=C.sPL)

    xt = P.sbuf("xt", [128, D], F32)
    yf = P.sbuf("yf", [128, D], F32)
    rbf = P.sbuf("rbf", [128, D], BF16)
    sq = P.sbuf("sq", [128, D], F32)
    cols = P.sbuf("cols", [128, 8], F32)
    r1T = P.sbuf("r1T", [128, NG * TPC], BF16)
    qtm = P.sbuf("qtm", [128, D], F32)
    qro = P.sbuf("qro", [128, D], BF16)
    lat = P.sbuf("lat", [128, 256], F32)
    latb = P.sbuf("latb", [128, 256], BF16)
    latT = P.sbuf("latT", [128, 2 * 128], BF16)
    ktm = P.sbuf("ktm", [128, 256], F32)
    kro = P.sbuf("kro", [128, 256], BF16)
    qT_sb = P.sbuf("qT_sb", [128, NG * TPC], BF16)
    kT_sb = P.sbuf("kT_sb", [128, 2 * TPC], BF16)
    va_sb = P.sbuf("va_sb", [128, NT * 260], BF16)
    costab = P.sbuf("costab", [128, 32], F32)
    sintab = P.sbuf("sintab", [128, 32], F32)
    invf_bc = P.sbuf("invf_bc", [128, 32], F32)
    invf_r = P.sbuf("invf_r", [1, 32], F32)
    rtmp = P.sbuf("rtmp", [128, 128], F32)
    irnd = P.sbuf("irnd", [128, 32], I32)
    poscol = P.sbuf("poscol", [128, 2], F32)
    qoffc = P.sbuf("qoffc", [128, 1], F32)
    ps_m = [P.psum(f"ps_m{j}", [128, 512], F32) for j in range(4)]
    pst = P.psum("pst", [128, 128], BF16)
    pstf = ps_m[3][:, 0:128]

    import math as _math
    P.on("gpsimd", lambda e: e.iota(rtmp[0:1, 0:32], pattern=[[1, 32]],
                                    base=0, channel_multiplier=0,
                                    allow_small_or_imprecise_dtypes=True),
         inc=C.sPL)
    P.on("scalar", lambda e: e.activation(
        rtmp[0:1, 32:64], rtmp[0:1, 0:32], AF.Exp,
        scale=-_math.log(1000000.0) / 32.0), wait=[C.sPL], inc=C.sACT)
    P.on("vector", lambda e: e.tensor_copy(invf_r[0:1, :],
                                           rtmp[0:1, 32:64]),
         wait=[C.sACT], inc=C.sVE)
    P.on("tensor", lambda e: e.matmul(
        pstf[:, 0:32], ones_row_f1[0:1, :], invf_r[0:1, :],
        start=True, stop=True), wait=[C.sVE, C.sPL], inc=C.sPE)
    P.on("vector", lambda e: e.tensor_copy(invf_bc[:, :], pstf[:, 0:32]),
         wait=[C.sPE], inc=C.sVE)
    P.on("gpsimd", lambda e: e.iota(poscol[:, 0:1], pattern=[[0, 1]],
                                    base=0, channel_multiplier=1,
                                    allow_small_or_imprecise_dtypes=True),
         inc=C.sPL)
    P.on("tensor", lambda e: e.matmul(
        pstf[:, 0:1], ones_row_f1[0:1, :], qoff[0:1, 0:1],
        start=True, stop=True), wait=[C.sPL, C.sX, C.sVE], inc=C.sPE)
    P.on("vector", lambda e: e.tensor_copy(qoffc[:, :], pstf[:, 0:1]),
         wait=[C.sPE], inc=C.sVE)

    for i in range(NT):
        C.load(x[i * 128:(i + 1) * 128, :], xt[:], wait=[C.sVE, C.sACT])
        C.ln_tile(xt[:], yf[:], sq[:], cols)
        emit_rms_scale(P, C, yf[:, :], rbf[:, :], cols, sq, n1c, pst,
                       r1T, i)
        P.on("vector", lambda e, i=i: e.tensor_scalar(
            poscol[:, 1:2], poscol[:, 0:1], 1.0, float(i * 128),
            ALU.mult, ALU.add), wait=[C.sVE, C.sPL, C.sX], inc=C.sVE)
        P.on("vector", lambda e: e.tensor_tensor(
            poscol[:, 1:2], poscol[:, 1:2], qoffc[:, :], ALU.add),
             wait=[C.sVE, C.sX], inc=C.sVE)
        emit_rope_tables(P, C, costab[:, :], sintab[:, :],
                         poscol[:, 1:2], invf_bc, rtmp, irnd)
        for oh in range(2):
            pm = ps_m[oh]
            for ic in range(NG):
                P.on("tensor", lambda e, i=i, oh=oh, ic=ic, pm=pm: e.matmul(
                    pm[:, :],
                    r1T[:, ic * TPC + i * 128: ic * TPC + (i + 1) * 128],
                    wqt[:, ic * D + oh * 512: ic * D + oh * 512 + 512],
                    start=(ic == 0), stop=(ic == NG - 1)),
                     wait=[C.sW, C.sVE], inc=C.sPE)
            P.on("vector", lambda e, oh=oh, pm=pm: e.tensor_copy(
                qtm[:, oh * 512: oh * 512 + 512], pm[:, :]),
                 wait=[C.sPE, C.sVE, C.sACT], inc=C.sVE)
        emit_rope_il(P, C, qtm, qro, costab, sintab, rtmp, 16, 64)
        emit_qknorm(P, C, qro, sq, cols, 16, 64)
        for g in range(NG):
            P.on("tensor", lambda e, g=g: e.transpose(
                pst[:, :], qro[:, g * 128:(g + 1) * 128], C.identb[:]),
                 wait=[C.sVE], inc=C.sPE)
            P.on("vector", lambda e, i=i, g=g: e.tensor_copy(
                qT_sb[:, g * TPC + i * 128: g * TPC + (i + 1) * 128],
                pst[:, :]), wait=[C.sPE], inc=C.sVE)
        pm = ps_m[2]
        for ic in range(NG):
            P.on("tensor", lambda e, i=i, ic=ic, pm=pm: e.matmul(
                pm[:, 0:256],
                r1T[:, ic * TPC + i * 128: ic * TPC + (i + 1) * 128],
                kvdt[:, ic * 256:(ic + 1) * 256],
                start=(ic == 0), stop=(ic == NG - 1)),
                 wait=[C.sW, C.sVE], inc=C.sPE)
        P.on("vector", lambda e, pm=pm: e.tensor_copy(latb[:, :],
                                                      pm[:, 0:256]),
             wait=[C.sPE, C.sVE], inc=C.sVE)
        for g in range(2):
            P.on("tensor", lambda e, g=g: e.transpose(
                pst[:, :], latb[:, g * 128:(g + 1) * 128], C.identb[:]),
                 wait=[C.sVE], inc=C.sPE)
            P.on("vector", lambda e, g=g: e.tensor_copy(
                latT[:, g * 128:(g + 1) * 128], pst[:, :]),
                 wait=[C.sPE], inc=C.sVE)
        pm = ps_m[2]
        for ic in range(2):
            P.on("tensor", lambda e, ic=ic, pm=pm: e.matmul(
                pm[:, 0:256], latT[:, ic * 128:(ic + 1) * 128],
                kupt[:, ic * 256:(ic + 1) * 256],
                start=(ic == 0), stop=(ic == 1)),
                 wait=[C.sW, C.sVE], inc=C.sPE)
        P.on("vector", lambda e, pm=pm: e.tensor_copy(ktm[:, :],
                                                      pm[:, 0:256]),
             wait=[C.sPE, C.sVE], inc=C.sVE)
        emit_rope_il(P, C, ktm, kro, costab, sintab, rtmp, 4, 64)
        emit_qknorm(P, C, kro, sq, cols, 4, 64)
        for g in range(2):
            P.on("tensor", lambda e, g=g: e.transpose(
                pst[:, :], kro[:, g * 128:(g + 1) * 128], C.identb[:]),
                 wait=[C.sVE], inc=C.sPE)
            P.on("vector", lambda e, i=i, g=g: e.tensor_copy(
                kT_sb[:, g * TPC + i * 128: g * TPC + (i + 1) * 128],
                pst[:, :]), wait=[C.sPE], inc=C.sVE)
        pm = ps_m[1]
        for ic in range(2):
            P.on("tensor", lambda e, ic=ic, pm=pm: e.matmul(
                pm[:, 0:256], latT[:, ic * 128:(ic + 1) * 128],
                vupt[:, ic * 256:(ic + 1) * 256],
                start=(ic == 0), stop=(ic == 1)),
                 wait=[C.sW, C.sVE], inc=C.sPE)
        for h in range(4):
            P.on("vector", lambda e, i=i, h=h, pm=pm: e.tensor_copy(
                va_sb[:, i * 260 + h * 65: i * 260 + h * 65 + 64],
                pm[:, h * 64:(h + 1) * 64]),
                 wait=[C.sPE, C.sVE], inc=C.sVE)
    P.on("gpsimd", lambda e: e.memset(
        va_sb[:].rearrange("p (i h o) -> p i h o", i=NT, h=4)
        [:, :, :, 64:65], 1.0), wait=[C.sVE], inc=C.sPL)
    C.store(qt_o.rearrange("(g p) t -> p g t", p=128),
            qT_sb[:].rearrange("p (g t) -> p g t", g=NG), wait=[C.sVE])
    C.store(kt_o.rearrange("(g p) t -> p g t", p=128),
            kT_sb[:].rearrange("p (g t) -> p g t", g=2), wait=[C.sVE])
    C.store(va_o.rearrange("(i p) c -> p i c", p=128),
            va_sb[:].rearrange("p (i c) -> p i c", i=NT),
            wait=[C.sVE, C.sPL])
    P.barrier_wait("gpsimd", C.sOUT)
    return P




def build_l3l_pre():
    P = Prog()
    z1 = P.dram("z1", [TPC, D], F32, kind="ExternalInput").ap()
    lwq = P.dram("l_wq", [D, D], F32, kind="ExternalInput").ap()
    lwk = P.dram("l_wk", [D, 256], F32, kind="ExternalInput").ap()
    lwv = P.dram("l_wv", [D, 256], F32, kind="ExternalInput").ap()
    n2 = P.dram("norm2_w", [128, NG], F32, kind="ExternalInput").ap()
    qoff_i = P.dram("qoff", [1, 1], F32, kind="ExternalInput").ap()
    qt_o = P.dram("qT", [D, TPC], BF16, kind="ExternalOutput").ap()
    kt_o = P.dram("kT", [256, TPC], BF16, kind="ExternalOutput").ap()
    va_o = P.dram("Va", [TPC, 4 * 65], BF16, kind="ExternalOutput").ap()
    C = Common(P)
    wqt = P.sbuf("wqt", [128, NG * D], BF16)
    wkt = P.sbuf("wkt", [128, NG * 256], BF16)
    wvt = P.sbuf("wvt", [128, NG * 256], BF16)
    C.load_w_direct(lwq, wqt, D, D)
    C.load_w_direct(lwk, wkt, D, 256)
    C.load_w_direct(lwv, wvt, D, 256)
    n2c = P.sbuf("n2c", [128, NG], F32)
    C.load(n2, n2c[:, :])
    qoff = P.sbuf("qoff_t", [1, 1], F32)
    C.load(qoff_i, qoff[0:1, :])
    ones_row_f1 = P.sbuf("ones_row_f1", [1, 128], F32)
    P.on("gpsimd", lambda e: e.memset(ones_row_f1[:], 1.0), inc=C.sPL)

    xt = P.sbuf("xt", [128, D], F32)
    rbf = P.sbuf("rbf", [128, D], BF16)
    sq = P.sbuf("sq", [128, D], F32)
    cols = P.sbuf("cols", [128, 8], F32)
    r2T = P.sbuf("r2T", [128, NG * TPC], BF16)
    qtm = P.sbuf("qtm", [128, D], F32)
    qro = P.sbuf("qro", [128, D], BF16)
    ktm = P.sbuf("ktm", [128, 256], F32)
    kro = P.sbuf("kro", [128, 256], BF16)
    qT_sb = P.sbuf("qT_sb", [128, NG * TPC], BF16)
    kT_sb = P.sbuf("kT_sb", [128, 2 * TPC], BF16)
    va_sb = P.sbuf("va_sb", [128, NT * 260], BF16)
    costab = P.sbuf("costab", [128, 32], F32)
    sintab = P.sbuf("sintab", [128, 32], F32)
    invf_bc = P.sbuf("invf_bc", [128, 32], F32)
    invf_r = P.sbuf("invf_r", [1, 32], F32)
    rtmp = P.sbuf("rtmp", [128, 128], F32)
    irnd = P.sbuf("irnd", [128, 32], I32)
    poscol = P.sbuf("poscol", [128, 2], F32)
    qoffc = P.sbuf("qoffc", [128, 1], F32)
    ps_m = [P.psum(f"ps_m{j}", [128, 512], F32) for j in range(4)]
    pst = P.psum("pst", [128, 128], BF16)
    pstf = ps_m[3][:, 0:128]

    import math as _math
    P.on("gpsimd", lambda e: e.iota(rtmp[0:1, 0:32], pattern=[[1, 32]],
                                    base=0, channel_multiplier=0,
                                    allow_small_or_imprecise_dtypes=True),
         inc=C.sPL)
    P.on("scalar", lambda e: e.activation(
        rtmp[0:1, 32:64], rtmp[0:1, 0:32], AF.Exp,
        scale=-_math.log(10000.0) / 32.0), wait=[C.sPL], inc=C.sACT)
    P.on("vector", lambda e: e.tensor_copy(invf_r[0:1, :],
                                           rtmp[0:1, 32:64]),
         wait=[C.sACT], inc=C.sVE)
    P.on("tensor", lambda e: e.matmul(
        pstf[:, 0:32], ones_row_f1[0:1, :], invf_r[0:1, :],
        start=True, stop=True), wait=[C.sVE, C.sPL], inc=C.sPE)
    P.on("vector", lambda e: e.tensor_copy(invf_bc[:, :], pstf[:, 0:32]),
         wait=[C.sPE], inc=C.sVE)
    P.on("gpsimd", lambda e: e.iota(poscol[:, 0:1], pattern=[[0, 1]],
                                    base=0, channel_multiplier=1,
                                    allow_small_or_imprecise_dtypes=True),
         inc=C.sPL)
    P.on("tensor", lambda e: e.matmul(
        pstf[:, 0:1], ones_row_f1[0:1, :], qoff[0:1, 0:1],
        start=True, stop=True), wait=[C.sPL, C.sX, C.sVE], inc=C.sPE)
    P.on("vector", lambda e: e.tensor_copy(qoffc[:, :], pstf[:, 0:1]),
         wait=[C.sPE], inc=C.sVE)

    for i in range(NT):
        C.load(z1[i * 128:(i + 1) * 128, :], xt[:], wait=[C.sVE, C.sACT])
        emit_rms_scale(P, C, xt[:, :], rbf[:, :], cols, sq, n2c, pst,
                       r2T, i)
        P.on("vector", lambda e, i=i: e.tensor_scalar(
            poscol[:, 1:2], poscol[:, 0:1], 1.0, float(i * 128),
            ALU.mult, ALU.add), wait=[C.sVE, C.sPL, C.sX], inc=C.sVE)
        P.on("vector", lambda e: e.tensor_tensor(
            poscol[:, 1:2], poscol[:, 1:2], qoffc[:, :], ALU.add),
             wait=[C.sVE, C.sX], inc=C.sVE)
        emit_rope_tables(P, C, costab[:, :], sintab[:, :],
                         poscol[:, 1:2], invf_bc, rtmp, irnd)
        for oh in range(2):
            pm = ps_m[oh]
            for ic in range(NG):
                P.on("tensor", lambda e, i=i, oh=oh, ic=ic, pm=pm: e.matmul(
                    pm[:, :],
                    r2T[:, ic * TPC + i * 128: ic * TPC + (i + 1) * 128],
                    wqt[:, ic * D + oh * 512: ic * D + oh * 512 + 512],
                    start=(ic == 0), stop=(ic == NG - 1)),
                     wait=[C.sW, C.sVE], inc=C.sPE)
            P.on("vector", lambda e, oh=oh, pm=pm: e.tensor_copy(
                qtm[:, oh * 512: oh * 512 + 512], pm[:, :]),
                 wait=[C.sPE, C.sVE, C.sACT], inc=C.sVE)
        emit_rope_il(P, C, qtm, qro, costab, sintab, rtmp, 16, 64)
        emit_qknorm(P, C, qro, sq, cols, 16, 64)
        for g in range(NG):
            P.on("tensor", lambda e, g=g: e.transpose(
                pst[:, :], qro[:, g * 128:(g + 1) * 128], C.identb[:]),
                 wait=[C.sVE], inc=C.sPE)
            P.on("vector", lambda e, i=i, g=g: e.tensor_copy(
                qT_sb[:, g * TPC + i * 128: g * TPC + (i + 1) * 128],
                pst[:, :]), wait=[C.sPE], inc=C.sVE)
        pm = ps_m[2]
        for ic in range(NG):
            P.on("tensor", lambda e, i=i, ic=ic, pm=pm: e.matmul(
                pm[:, 0:256],
                r2T[:, ic * TPC + i * 128: ic * TPC + (i + 1) * 128],
                wkt[:, ic * 256:(ic + 1) * 256],
                start=(ic == 0), stop=(ic == NG - 1)),
                 wait=[C.sW, C.sVE], inc=C.sPE)
        P.on("vector", lambda e, pm=pm: e.tensor_copy(ktm[:, :],
                                                      pm[:, 0:256]),
             wait=[C.sPE, C.sVE], inc=C.sVE)
        emit_rope_il(P, C, ktm, kro, costab, sintab, rtmp, 4, 64)
        emit_qknorm(P, C, kro, sq, cols, 4, 64)
        for g in range(2):
            P.on("tensor", lambda e, g=g: e.transpose(
                pst[:, :], kro[:, g * 128:(g + 1) * 128], C.identb[:]),
                 wait=[C.sVE], inc=C.sPE)
            P.on("vector", lambda e, i=i, g=g: e.tensor_copy(
                kT_sb[:, g * TPC + i * 128: g * TPC + (i + 1) * 128],
                pst[:, :]), wait=[C.sPE], inc=C.sVE)
        pm = ps_m[1]
        for ic in range(NG):
            P.on("tensor", lambda e, i=i, ic=ic, pm=pm: e.matmul(
                pm[:, 0:256],
                r2T[:, ic * TPC + i * 128: ic * TPC + (i + 1) * 128],
                wvt[:, ic * 256:(ic + 1) * 256],
                start=(ic == 0), stop=(ic == NG - 1)),
                 wait=[C.sW, C.sVE], inc=C.sPE)
        for h in range(4):
            P.on("vector", lambda e, i=i, h=h, pm=pm: e.tensor_copy(
                va_sb[:, i * 260 + h * 65: i * 260 + h * 65 + 64],
                pm[:, h * 64:(h + 1) * 64]),
                 wait=[C.sPE, C.sVE], inc=C.sVE)
    P.on("gpsimd", lambda e: e.memset(
        va_sb[:].rearrange("p (i h o) -> p i h o", i=NT, h=4)
        [:, :, :, 64:65], 1.0), wait=[C.sVE], inc=C.sPL)
    C.store(qt_o.rearrange("(g p) t -> p g t", p=128),
            qT_sb[:].rearrange("p (g t) -> p g t", g=NG), wait=[C.sVE])
    C.store(kt_o.rearrange("(g p) t -> p g t", p=128),
            kT_sb[:].rearrange("p (g t) -> p g t", g=2), wait=[C.sVE])
    C.store(va_o.rearrange("(i p) c -> p i c", p=128),
            va_sb[:].rearrange("p (i c) -> p i c", i=NT),
            wait=[C.sVE, C.sPL])
    P.barrier_wait("gpsimd", C.sOUT)
    return P


def build_l3l_main():
    P = Prog()
    NH_ = 16
    SCALE = 0.125
    KV = TPC + 256
    x = P.dram("x", [TPC, D], F32, kind="ExternalInput").ap()
    z1 = P.dram("z1", [TPC, D], F32, kind="ExternalInput").ap()
    qt_i = P.dram("qT", [D, TPC], BF16, kind="ExternalInput").ap()
    kt_i = P.dram("kTall", [D, KV], BF16, kind="ExternalInput").ap()
    va_i = P.dram("Vall", [KV, 1040], BF16, kind="ExternalInput").ap()
    wo = P.dram("Wo", [D, D], F32, kind="ExternalInput").ap()
    xout = P.dram("xout", [TPC, D], F32, kind="ExternalOutput").ap()
    C = Common(P)
    woT = P.sbuf("woT", [128, NG * D], BF16)
    C.load_w_direct(wo, woT, D, D)
    ps8 = [P.psum(f"ps8_{j}", [128, 512], F32) for j in range(6)]
    pstf = ps8[2][:, 0:128]
    qT = P.sbuf("qT_t", [128, NG * TPC], BF16)
    C.load(qt_i.rearrange("(g p) t -> p g t", p=128),
           qT[:].rearrange("p (g t) -> p g t", g=NG))
    bcol = P.sbuf("bcol", [128, 1], F32)
    masks = P.sbuf("masks", [128, 6 * 512], BF16)

    def setup(g):
        g.memset(bcol[:], -8.0)
        g.memset(masks[:], 1.0)
        last = None
        for mi, krel in enumerate((-256, -128, 0, 128, 256, 384)):
            g.affine_select(out=masks[:, mi * 512:(mi + 1) * 512],
                            in_=masks[:, mi * 512:(mi + 1) * 512],
                            compare_op=ALU.is_ge, fill=0.0, base=-krel,
                            pattern=[[1, 512]], channel_multiplier=-1)
            last = g.affine_select(
                out=masks[:, mi * 512:(mi + 1) * 512],
                in_=masks[:, mi * 512:(mi + 1) * 512],
                compare_op=ALU.is_ge, fill=0.0, base=krel + 255,
                pattern=[[-1, 512]], channel_multiplier=1)
        return last
    P.on("gpsimd", setup, inc=C.sPL)

    ktc = [P.sbuf(f"ktc{j}", [128, NG * 512], BF16) for j in range(2)]
    vac = [P.sbuf(f"vac{j}", [128, 4 * 1040], BF16) for j in range(2)]
    ptk = P.sbuf("ptk", [128, 4 * 512], BF16)
    accS = P.sbuf("accS", [128, 9 * 512], F32)
    accS2 = P.sbuf("accS2", [128, 7 * 512], F32)
    rlb = P.sbuf("rlb", [64, 512], F32)
    onesall = P.sbuf("onesall", [128, 64], F32)
    P.on("gpsimd", lambda e: e.memset(onesall[:], 1.0), inc=C.sPL)
    mrg = P.sbuf("mrg", [128, NG * TPC], BF16)
    fT = P.sbuf("fT", [128, NG * 512], F32)
    xt = P.sbuf("xt", [128, D], F32)
    zt = P.sbuf("zt", [128, D], F32)

    def accS_ap(h):
        if h < 9:
            return accS[:, h * 512:(h + 1) * 512]
        return accS2[:, (h - 9) * 512:(h - 8) * 512]

    for qb in range(NTB):
        for kc in range(2):
            sb = kc % 2
            k0 = qb * 512 + kc * 512
            nks = 4 if kc == 0 else 2
            kw = 512 if kc == 0 else 256
            C.load_cast(
                kt_i[:, k0:k0 + kw].rearrange("(g p) t -> p g t", p=128),
                ktc[sb][:].rearrange("p (g t) -> p g t", g=NG)[:, :, 0:kw],
                wait=[C.sPE], eng="sync")
            C.load_cast(
                va_i[k0:k0 + kw, :].rearrange("(s p) c -> p s c", p=128),
                vac[sb][:].rearrange("p (s c) -> p s c", s=4)
                [:, 0:kw // 128, :], wait=[C.sPE], eng="gpsimd")
            for h in range(NH_):
                hrow = 64 * (h % 2)
                hg = h // 2
                for ks in range(nks):
                    P.on("tensor", lambda e, h=h, qb=qb, ks=ks, sb=sb,
                         hrow=hrow, hg=hg: e.matmul(
                         ps8[ks][:, :],
                         ktc[sb][hrow:hrow + 64,
                                 hg * 512 + ks * 128:
                                 hg * 512 + (ks + 1) * 128],
                         qT[hrow:hrow + 64,
                            hg * TPC + qb * 512: hg * TPC + (qb + 1) * 512],
                         start=True, stop=True),
                         wait=[C.sW, C.sX, C.sVE, C.sACT], inc=C.sPE)
                    P.on("scalar", lambda e, ks=ks: e.activation(
                        ptk[:, ks * 512:(ks + 1) * 512], ps8[ks][:, :],
                        AF.Exp, scale=SCALE, bias=bcol[:, 0:1]),
                         wait=[C.sPE, C.sVE, C.sPL], inc=C.sACT)
                    mi = kc * 4 + ks
                    P.on("vector", lambda e, ks=ks, mi=mi: e.tensor_tensor(
                        ptk[:, ks * 512:(ks + 1) * 512],
                        ptk[:, ks * 512:(ks + 1) * 512],
                        masks[:, mi * 512:(mi + 1) * 512], ALU.mult),
                         wait=[C.sACT, C.sVE, C.sPL], inc=C.sVE)
                for ks in range(nks):
                    P.on("tensor", lambda e, h=h, ks=ks, sb=sb, nks=nks:
                         e.matmul(
                         ps8[4][0:65, :],
                         vac[sb][:, ks * 1040 + h * 65:
                                 ks * 1040 + (h + 1) * 65],
                         ptk[:, ks * 512:(ks + 1) * 512],
                         start=(ks == 0), stop=(ks == nks - 1)),
                         wait=[C.sACT, C.sVE], inc=C.sPE)
                if kc == 0:
                    P.on("vector", lambda e, h=h: e.tensor_copy(
                        accS_ap(h)[0:65, :], ps8[4][0:65, :]),
                         wait=[C.sPE, C.sVE], inc=C.sVE)
                else:
                    P.on("vector", lambda e, h=h: e.tensor_tensor(
                        accS_ap(h)[0:65, :], accS_ap(h)[0:65, :],
                        ps8[4][0:65, :], ALU.add),
                         wait=[C.sPE, C.sVE], inc=C.sVE)
        for h in range(NH_):
            P.on("vector", lambda e, h=h: e.reciprocal(
                accS_ap(h)[64:65, :], accS_ap(h)[64:65, :]),
                 wait=[C.sVE], inc=C.sVE)
            P.on("tensor", lambda e, h=h: e.matmul(
                ps8[5][0:64, :], onesall[64:65, 0:64],
                accS_ap(h)[64:65, :], start=True, stop=True),
                 wait=[C.sVE, C.sPL], inc=C.sPE)
            P.on("vector", lambda e, h=h: e.tensor_copy(
                rlb[0:64, :], ps8[5][0:64, :]), wait=[C.sPE, C.sVE],
                 inc=C.sVE)
            hrow = 64 * (h % 2)
            hg = h // 2
            P.on("vector", lambda e, h=h, hrow=hrow, hg=hg, qb=qb:
                 e.tensor_tensor(
                 mrg[hrow:hrow + 64,
                     hg * TPC + qb * 512: hg * TPC + (qb + 1) * 512],
                 accS_ap(h)[0:64, :], rlb[0:64, :], ALU.mult),
                 wait=[C.sVE], inc=C.sVE)
    for tb in range(NTB):
        for og in range(NG):
            pm = ps8[og % 2]
            for ic in range(NG):
                P.on("tensor", lambda e, og=og, tb=tb, ic=ic, pm=pm:
                     e.matmul(
                     pm[:, :],
                     woT[:, ic * D + og * 128: ic * D + (og + 1) * 128],
                     mrg[:, ic * TPC + tb * 512: ic * TPC + tb * 512 + 512],
                     start=(ic == 0), stop=(ic == NG - 1)),
                     wait=[C.sW, C.sVE], inc=C.sPE)
            P.on("vector", lambda e, og=og, pm=pm: e.tensor_copy(
                fT[:, og * 512:(og + 1) * 512], pm[:, :]),
                 wait=[C.sPE, C.sVE], inc=C.sVE)
        for ii in range(4):
            i = tb * 4 + ii
            C.load(x[i * 128:(i + 1) * 128, :], xt[:],
                   wait=[C.sVE, C.sOUT])
            C.load(z1[i * 128:(i + 1) * 128, :], zt[:],
                   wait=[C.sVE, C.sOUT])
            P.on("vector", lambda e: e.tensor_tensor(
                xt[:, :], xt[:, :], zt[:, :], ALU.add),
                 wait=[C.sX, C.sVE], inc=C.sVE)
            for g in range(NG):
                P.on("tensor", lambda e, ii=ii, g=g: e.transpose(
                    pstf[:, :],
                    fT[:, g * 512 + ii * 128: g * 512 + (ii + 1) * 128],
                    C.identf[:]), wait=[C.sVE], inc=C.sPE)
                P.on("vector", lambda e, g=g: e.scalar_tensor_tensor(
                    xt[:, g * 128:(g + 1) * 128], pstf[:, :], 1.0,
                    xt[:, g * 128:(g + 1) * 128], ALU.mult, ALU.add),
                     wait=[C.sPE, C.sX], inc=C.sVE)
            C.store(xout[i * 128:(i + 1) * 128, :], xt[:], wait=[C.sVE])
    P.barrier_wait("gpsimd", C.sOUT)
    return P


def build_l3y():
    P = Prog()
    x = P.dram("x", [TPC, D], F32, kind="ExternalInput").ap()
    y_o = P.dram("y", [TPC, D], F32, kind="ExternalOutput").ap()
    C = Common(P)
    xt = P.sbuf("xt", [128, D], F32)
    yf = P.sbuf("yf", [128, D], F32)
    sq = P.sbuf("sq", [128, D], F32)
    cols = P.sbuf("cols", [128, 8], F32)
    for i in range(NT):
        C.load(x[i * 128:(i + 1) * 128, :], xt[:],
               wait=[C.sVE, C.sACT, C.sOUT])
        C.ln_tile(xt[:], yf[:], sq[:], cols)
        C.store(y_o[i * 128:(i + 1) * 128, :], yf[:], wait=[C.sVE])
    P.barrier_wait("gpsimd", C.sOUT)
    return P


def l3a_layer(xs, w, pfx):
    in1 = []
    for c in range(NCORES):
        qoff = np.zeros((1, 1), np.float32)
        qoff[0, 0] = (c % 4) * TPC
        in1.append({"x": xs[c], "g_wq": w[f"{pfx}_g_wq"],
                    "g_kv_down": w[f"{pfx}_g_kv_down"],
                    "g_k_up": w[f"{pfx}_g_k_up"],
                    "g_v_up": w[f"{pfx}_g_v_up"],
                    "norm1_w": cm(w[f"{pfx}_norm1_w"]), "qoff": qoff})
    r = _run(build_l3a_pre, in1, "l3a_pre")
    qts = [r.results[c]["qT"] for c in range(NCORES)]
    kts = [r.results[c]["kT"] for c in range(NCORES)]
    vas = [r.results[c]["Va"] for c in range(NCORES)]
    r = _run(build_l3y, [{"x": xs[c]} for c in range(NCORES)], "l3y")
    ys = [r.results[c]["y"] for c in range(NCORES)]
    qn_const = np.full((TPC, 16), 64.0, np.float32)
    kn_const = np.full((4 * 16, 1), 64.0, np.float32)
    woTh = np.ascontiguousarray(w[f"{pfx}_g_wo"].T)
    bo0 = np.zeros((128, NG), np.float32)
    z1s = [None] * NCORES
    for qg in range(4):
        kvlen = (qg + 1) * TPC
        maps = []
        for c in (qg, qg + 4):
            b0 = (c // 4) * 4
            nq = qg + 1
            ktall = np.concatenate([kts[b0 + j] for j in range(nq)], axis=1)
            ktall = np.ascontiguousarray(np.repeat(
                ktall.reshape(4, 64, kvlen), 4, axis=0).reshape(1024, kvlen))
            vall = np.concatenate([vas[b0 + j] for j in range(nq)], axis=0)
            vall = np.ascontiguousarray(np.repeat(
                vall.reshape(kvlen, 4, 65), 4, axis=1).reshape(kvlen, 1040))
            qoff = np.zeros((1, 1), np.float32)
            qoff[0, 0] = (c % 4) * TPC
            maps.append({"x": ys[c], "qT": qts[c], "kTall": ktall,
                         "Vall": vall, "qn2": qn_const, "knmax": kn_const,
                         "Wo": woTh, "bo": bo0, "qoff": qoff})
        name = f"l1a_main{qg}"
        if name not in _CACHE:
            _CACHE[name] = build_l1a_main(kvlen).finalize()
        r = run_bass_kernel_spmd(_CACHE[name], maps, core_ids=[0, 1])
        z1s[qg] = r.results[0]["xout"]
        z1s[qg + 4] = r.results[1]["xout"]
    in3 = []
    for c in range(NCORES):
        qoff = np.zeros((1, 1), np.float32)
        qoff[0, 0] = (c % 4) * TPC
        in3.append({"z1": z1s[c], "l_wq": w[f"{pfx}_l_wq"],
                    "l_wk": w[f"{pfx}_l_wk"], "l_wv": w[f"{pfx}_l_wv"],
                    "norm2_w": cm(w[f"{pfx}_norm2_w"]), "qoff": qoff})
    r = _run(build_l3l_pre, in3, "l3l_pre")
    qtl = [r.results[c]["qT"] for c in range(NCORES)]
    ktl = [r.results[c]["kT"] for c in range(NCORES)]
    val = [r.results[c]["Va"] for c in range(NCORES)]
    in4 = []
    for c in range(NCORES):
        if c % 4 != 0:
            ktprev = ktl[c - 1][:, -256:]
            vaprev = val[c - 1][-256:, :]
        else:
            ktprev = np.zeros((256, 256), ktl[c].dtype)
            vaprev = np.zeros((256, 260), val[c].dtype)
        ktall = np.concatenate([ktprev, ktl[c]], axis=1)
        ktall = np.ascontiguousarray(np.repeat(
            ktall.reshape(4, 64, TPC + 256), 4, axis=0).reshape(
            1024, TPC + 256))
        vall = np.concatenate([vaprev, val[c]], axis=0)
        vall = np.ascontiguousarray(np.repeat(
            vall.reshape(TPC + 256, 4, 65), 4, axis=1).reshape(
            TPC + 256, 1040))
        in4.append({"x": xs[c], "z1": z1s[c], "qT": qtl[c],
                    "kTall": ktall, "Vall": vall,
                    "Wo": np.ascontiguousarray(w[f"{pfx}_l_wo"])})
    r = _run(build_l3l_main, in4, "l3l_main")
    return [r.results[c]["xout"] for c in range(NCORES)]


def ffn0_layer(xs, w, pfx, ncycles=3):
    in_maps = []
    for c in range(NCORES):
        b = c // 4
        b2r = np.zeros((1, 2), np.float32); b2r[0, b] = 1.0
        b2c = np.zeros((2, 1), np.float32); b2c[b, 0] = 1.0
        m = {"x": xs[c], "bsel2r": b2r, "bsel2c": b2c,
             "smal": np.array([[w[f"{pfx}_cn_b4"][0],
                                w[f"{pfx}_gc_b3"][0],
                                w[f"{pfx}_adaptive_range"][0], 0.0]],
                              np.float32)}
        for k in ["cn_w1", "cn_w2", "cn_w3", "gc_w1", "gc_w2"]:
            m[k] = w[f"{pfx}_{k}"]
        m["cn_w4"] = cm(w[f"{pfx}_cn_w4"])
        m["gc_w3"] = cm(w[f"{pfx}_gc_w3"])
        for k in ["cn_b1", "cn_b2", "cn_b3", "gc_b1", "gc_b2"]:
            m[k] = cm(w[f"{pfx}_{k}"])
        in_maps.append(m)
    r = _run(lambda: build_ffn0_main(ncycles), in_maps,
             f"ffn0_main{ncycles}")
    return [r.results[c]["xout"] for c in range(NCORES)]


def kernel(**inputs):
    w = {k: np.asarray(v) for k, v in inputs.items()}
    xs = _shard_x(w["L0_attn_x"])
    xs = ffn0_layer(xs, w, "L0_ffn")
    xs = ssm_layer(xs, w, "L0_attn")
    xs = ffn0_layer(xs, w, "L0_ffn")
    xs = moe1_layer(xs, w, "L1_ffn")
    xs = l1a_layer(xs, w, "L1_attn")
    xs = moe1_layer(xs, w, "L1_ffn")
    xs = tffn2_layer(xs, w, "L2_ffn")
    xs = l2a_layer(xs, w, "L2_attn")
    xs = tffn2_layer(xs, w, "L2_ffn")
    xs = moe3_layer(xs, w, "L3_ffn")
    xs = l3a_layer(xs, w, "L3_attn")
    xs = moe3_layer(xs, w, "L3_ffn")
    return _gather_x(xs)
